# version 29
# baseline (speedup 1.0000x reference)
import contextlib
import numpy as np
import ml_dtypes
import concourse.bass as bass
import concourse.mybir as mybir
from concourse.bass_utils import run_bass_kernel_spmd

F32 = mybir.dt.float32
BF16 = mybir.dt.bfloat16
AF = mybir.ActivationFunctionType
ALU = mybir.AluOpType
AX = mybir.AxisListType

H = 16
NS = 8
S = 2048
D = 2048
HD = 128
NCH = 16
NT = 4
INVSQ = float(1.0 / np.sqrt(128.0))
SQ128 = float(np.sqrt(128.0))
NEG = -1e30
UBCLAMP = 25.0

DEBUG_TAPS = ()
DEBUG_SBUF = ()
BUILD_UPTO = 9


class Pipe:
    ENGINES = ("pe", "act", "dve", "pool", "sync", "gp")
    DMA_ENG = {"sync", "gp"}

    def __init__(self, nc, stack):
        self.nc = nc
        self.sems = {e: stack.enter_context(nc.semaphore("sem_" + e)) for e in self.ENGINES}
        self.steps = []
        self.res_w = {}
        self.res_r = {}

    def step(self, eng, fn, reads=(), writes=(), deps=(), tag="", dma=False):
        sid = len(self.steps)
        dep_set = {}

        def add_dep(d):
            if d is None:
                return
            assert d < sid
            de = self.steps[d]["eng"]
            if de == eng == "pe":
                return
            dep_set[de] = max(dep_set.get(de, -1), d)

        for d in deps:
            add_dep(d)
        for r in list(reads) + list(writes):
            for _, d in self.res_w.get(r, {}).items():
                add_dep(d)
        for r in writes:
            for _, d in self.res_r.get(r, {}).items():
                add_dep(d)
        self.steps.append({"eng": eng, "fn": fn, "deps": dep_set, "tag": tag,
                           "id": sid,
                           "inc": 16 if (eng in self.DMA_ENG or dma) else 1})
        for r in reads:
            self.res_r.setdefault(r, {})[eng] = sid
        for r in writes:
            self.res_w.setdefault(r, {})[eng] = sid
        return sid

    def barrier_deps(self):
        last = {}
        for st in self.steps:
            last[st["eng"]] = st["id"]
        return list(last.values())

    def emit(self, block):
        cnt = {e: 0 for e in self.ENGINES}
        sem_after = {}
        for st in self.steps:
            e = st["eng"]
            cnt[e] += st["inc"]
            sem_after[st["id"]] = cnt[e]

        def make_runner(names):
            def run(eng):
                watermark = {}
                for st in self.steps:
                    if st["eng"] not in names:
                        continue
                    for de, d in sorted(st["deps"].items()):
                        v = sem_after[d]
                        if watermark.get(de, 0) >= v:
                            continue
                        eng.wait_ge(self.sems[de], v)
                        watermark[de] = v
                    last = st["fn"](eng)
                    assert last is not None, f"step {st['tag']} returned None"
                    last.then_inc(self.sems[st["eng"]], st["inc"])
            return run

        for name, starter in (("pe", block.tensor), ("act", block.scalar),
                              ("dve", block.vector), ("sync", block.sync)):
            if any(st["eng"] == name for st in self.steps):
                starter(make_runner({name}))
        if any(st["eng"] in ("gp", "pool") for st in self.steps):
            block.gpsimd(make_runner({"gp", "pool"}))

        def final_wait(eng):
            for e in ("sync", "gp", "pe", "act", "dve", "pool"):
                if cnt[e]:
                    eng.wait_ge(self.sems[e], cnt[e])
        block.sync(final_wait)


def build_program(taps=()):
    nc = bass.Bass("TRN2", target_bir_lowering=False)

    def dram_in(name, shape, dtype=F32):
        return nc.dram_tensor(name, shape, dtype, kind="ExternalInput").ap()

    def dscratch(name, shape, dtype):
        kind = "ExternalOutput" if name in taps else "Internal"
        return nc.dram_tensor(name, shape, dtype, kind=kind).ap()

    io = {
        "xT": dram_in("xT", [D, S]),
        "xTb": dram_in("xTb", [D, S], BF16),
        "wqT": dram_in("wqT", [512, 8192], BF16),
        "wkT": dram_in("wkT", [512, 8192], BF16),
        "wvT": dram_in("wvT", [512, 8192], BF16),
        "woT": dram_in("woT", [1024, 4096], BF16),
        "wa_qa": dram_in("wa_qa", [512, 8192], BF16),
        "wa_ka": dram_in("wa_ka", [512, 8192], BF16),
        "wa_v": dram_in("wa_v", [512, 8192], BF16),
        "wa_low": dram_in("wa_low", [128, 4096], BF16),
        "wp": dram_in("wp", [1024, 4096], BF16),
        "pos": dram_in("pos", [H, NS, HD]), "dirs": dram_in("dirs", [H, NS, HD]),
        "lsc": dram_in("lsc", [H, NS]), "lam": dram_in("lam", [H, NS]),
        "ds": dram_in("ds", [1, 1]), "ps": dram_in("ps", [1, 1]),
        "gsa": dram_in("gsa", [1, 1]),
        "wstd": dram_in("wstd", [1, H]), "wrec": dram_in("wrec", [1, H]),
        "masksT": dram_in("masksT", [4, 128, 512], BF16),
        "ident": dram_in("ident", [128, 128]),
        "identb": dram_in("identb", [128, 128], BF16),
        "pats": dram_in("pats", [1, 256]),
        "rep16": dram_in("rep16", [H, 128]),
        "outT": nc.dram_tensor("outT", [D, S], F32, kind="ExternalOutput").ap(),
        "q_dram": dscratch("q_dram", [D, S], BF16),
        "k_dram": dscratch("k_dram", [D, S], BF16),
        "v_dram": dscratch("v_dram", [S, D], BF16),
        "w1_dram": dscratch("w1_dram", [H * 16, S], BF16),
        "w2_dram": dscratch("w2_dram", [H * 16, S], BF16),
        "x1_dram": dscratch("x1_dram", [D, S], F32),
        "x1b_dram": dscratch("x1b_dram", [D, S], BF16),
        "qf_dram": dscratch("qf_dram", [D, S], BF16),
        "kf_dram": dscratch("kf_dram", [D, S], BF16),
        "v2_dram": dscratch("v2_dram", [S, D], BF16),
    }

    with contextlib.ExitStack() as stack:
        sb = {}

        def alloc(name, shape, dtype):
            sb[name] = stack.enter_context(nc.sbuf_tensor(name, shape, dtype))

        alloc("masks", [128, 4 * 512], BF16)
        alloc("ident_f", [128, 128], F32)
        alloc("ident_b", [128, 128], BF16)
        alloc("ones_b", [128, 128], BF16)
        alloc("onescol_b", [128, 1], BF16)
        alloc("onesrow_f", [1, 128], F32)
        alloc("onescol8_f", [NS, 1], F32)
        alloc("pos_sb", [NS, H * HD], F32)
        alloc("dirs_sb", [NS, H * HD], F32)
        alloc("ls_sb", [H, NS], F32)
        alloc("lam_sb", [H, NS], F32)
        alloc("lamT", [NS, H], F32)
        alloc("ivs2", [NS, H], F32)
        alloc("scal", [1, 8], F32)
        alloc("dsv8", [NS, 1], F32)
        alloc("psv8", [NS, 1], F32)
        alloc("bvec", [128, 1], F32)
        alloc("ombvec", [128, 1], F32)
        alloc("lnb", [128, 2], F32)
        alloc("negub", [128, H], F32)
        alloc("wstd_row", [1, H], F32)
        alloc("wrec_row", [1, H], F32)
        alloc("srow", [1, 8 * H], F32)
        alloc("qsc", [128, H], F32)
        alloc("ksc", [128, H], F32)
        alloc("tailsc", [128, 4], F32)
        alloc("pats_sb", [1, 256], F32)
        alloc("rep16_sb", [H, 128], F32)
        alloc("col16", [H, 4], F32)
        alloc("r1q", [128, H * 64], BF16)
        alloc("r1k", [128, H * 64], BF16)
        alloc("rcq", [2, H * 64], BF16)
        alloc("rck", [2, H * 64], BF16)
        alloc("qsq_pair", [2, 1024], BF16)
        alloc("row_a", [1, 2048], F32)
        alloc("row_b", [1, 2048], F32)
        alloc("row_c", [1, 2048], F32)
        alloc("tmp8a", [NS, HD], F32)
        alloc("tmp8b", [NS, HD], F32)
        alloc("col8", [NS, 8], F32)
        alloc("mrow_t", [128, 512], F32)
        alloc("rrow_t", [128, 512], F32)
        alloc("big0", [128, H * S], BF16)
        alloc("scr_b", [128, 26 * 1024], BF16)
        alloc("scr_f", [128, 4 * 512], F32)
        alloc("w16a", [16, 2 * S], BF16)
        alloc("w16b", [16, 2 * S], BF16)
        alloc("featbuf", [8, 4 * 512], BF16)

        ps = {}
        for nm, shape, dt in (("A", [128, 1024], F32), ("B", [128, 1024], F32),
                              ("CD", [128, 1024], F32), ("EF", [128, 1024], F32)):
            ps[nm] = stack.enter_context(nc.psum_tensor("ps" + nm, shape, dt))
        ps["C"] = ps["CD"][:, 0:512]
        ps["D"] = ps["CD"][:, 512:1024]
        ps["E"] = ps["EF"][:, 0:512]
        ps["F"] = ps["EF"][:, 512:1024]

        pipe = Pipe(nc, stack)
        env = {"nc": nc, "io": io, "sb": sb, "ps": ps, "p": pipe}

        def bview(c0, c1):
            assert 0 <= c0 < c1 <= 26 * 1024
            return sb["scr_b"][:, c0:c1], [f"sbB{i}" for i in
                                           range(c0 // 1024, (c1 + 1023) // 1024)]

        def fview(c0, c1):
            assert 0 <= c0 < c1 <= 6 * 512
            return sb["scr_f"][:, c0:c1], [f"sfB{i}" for i in
                                           range(c0 // 512, (c1 + 511) // 512)]
        env["bview"] = bview
        env["fview"] = fview

        phase0(env)
        if BUILD_UPTO >= 1:
            phase1_ln(env, src=io["xTb"], use_entry=False)
        p2sids = None
        if BUILD_UPTO >= 2:
            p2sids = phase2_qkv(env)
        if BUILD_UPTO >= 4:
            phase4_attn(env, p2sids)
        if BUILD_UPTO >= 5:
            b16_sids = phase5_proj(env, w_dram=io["woT"], x_dram=io["xT"],
                                   dst=io["x1_dram"], mk_b16=True)
        if BUILD_UPTO >= 6:
            phase1_ln(env, src=io["x1b_dram"],
                      chunk_deps=lambda c: [b16_sids[(j, c)] for j in range(NT)])
        if BUILD_UPTO >= 7:
            phase7_ffnproj(env)
        if BUILD_UPTO >= 8:
            phase8_ffnattn(env)
        if BUILD_UPTO >= 9:
            phase5_proj(env, w_dram=io["wp"], x_dram=io["x1_dram"], dst=io["outT"])

        fin = pipe.barrier_deps()
        for nm in DEBUG_SBUF:
            t = sb[nm]
            shape = [t.shape[0], t.shape[1]]
            dt = t.dtype
            dbg = nc.dram_tensor("dbg_" + nm, shape, dt, kind="ExternalOutput").ap()
            pipe.step("sync", lambda e, t=t, dbg=dbg: e.dma_start(out=dbg, in_=t[:, :]),
                      deps=fin, tag="dbg." + nm)

        with nc.Block() as block:
            pipe.emit(block)

    return nc


def phase0(env):
    io, sb, ps, p = env["io"], env["sb"], env["ps"], env["p"]

    def ld(name, out, in_):
        p.step("sync", lambda e, o=out, i=in_: e.dma_start(out=o, in_=i),
               writes=[name], tag="p0.ld." + name)

    ld("masks", sb["masks"][:, :].rearrange("p (d n) -> p d n", n=512),
       io["masksT"].rearrange("d p n -> p d n"))
    ld("ident_f", sb["ident_f"][:], io["ident"])
    ld("ident_b", sb["ident_b"][:], io["identb"])
    ld("pos_sb", sb["pos_sb"][:, :].rearrange("n (h d) -> n h d", d=HD),
       io["pos"].rearrange("h n d -> n h d"))
    ld("dirs_sb", sb["dirs_sb"][:, :].rearrange("n (h d) -> n h d", d=HD),
       io["dirs"].rearrange("h n d -> n h d"))
    ld("ls_sb", sb["ls_sb"][:], io["lsc"])
    ld("lam_sb", sb["lam_sb"][:], io["lam"])
    ld("wstd_row", sb["wstd_row"][:], io["wstd"])
    ld("wrec_row", sb["wrec_row"][:], io["wrec"])
    ld("pats_sb", sb["pats_sb"][:], io["pats"])
    ld("rep16_sb", sb["rep16_sb"][:], io["rep16"])
    ld("scal0", sb["scal"][:, 0:1], io["ds"])
    ld("scal1", sb["scal"][:, 1:2], io["ps"])
    ld("scal2", sb["scal"][:, 2:3], io["gsa"])

    def msets(e):
        e.memset(sb["ones_b"][:], 1.0)
        e.memset(sb["onescol_b"][:], 1.0)
        e.memset(sb["onesrow_f"][:], 1.0)
        e.memset(sb["onescol8_f"][:], 1.0)
        e.memset(sb["qsq_pair"][:], 1.0)
        e.memset(sb["r1q"][:], 0.0)
        e.memset(sb["r1k"][:], 0.0)
        e.memset(sb["rcq"][:], 0.0)
        return e.memset(sb["rck"][:], 0.0)
    p.step("dve", msets, writes=["ones", "r1", "rc", "qsq_pair"], tag="p0.memset")

    p.step("act", lambda e: e.activation(out=sb["scal"][:, 3:4], in_=sb["scal"][:, 2:3],
                                         func=AF.Sigmoid),
           reads=["scal2"], writes=["scal3"], tag="p0.sig")

    def blend_a(e):
        e.tensor_scalar(out=sb["scal"][:, 3:4], in0=sb["scal"][:, 3:4],
                        scalar1=0.15, scalar2=0.03, op0=ALU.mult, op1=ALU.min)
        e.tensor_reduce(out=sb["scal"][:, 5:6], in_=sb["scal"][:, 1:2], axis=AX.X,
                        op=ALU.max, apply_absolute_value=True)
        return e.tensor_reduce(out=sb["scal"][:, 6:7], in_=sb["scal"][:, 0:1], axis=AX.X,
                               op=ALU.max, apply_absolute_value=True)
    p.step("dve", blend_a, reads=["scal3", "scal0", "scal1"], writes=["scal3", "scal5"],
           tag="p0.blend_a")

    def blend_b(e):
        e.tensor_scalar(out=sb["scal"][:, 4:5], in0=sb["scal"][:, 3:4],
                        scalar1=-1.0, scalar2=1.0, op0=ALU.mult, op1=ALU.add)
        return e.tensor_scalar(out=sb["scal"][:, 6:7], in0=sb["scal"][:, 6:7],
                               scalar1=1.0, scalar2=None, op0=ALU.add)
    p.step("dve", blend_b, reads=["scal3"], writes=["scal3", "scal5"], tag="p0.blend_b")
    p.step("dve", lambda e: e.tensor_mul(sb["scal"][:, 5:6], sb["scal"][:, 5:6],
                                         sb["scal"][:, 6:7]),
           reads=["scal3"], writes=["scal5"], tag="p0.blend_c")

    def tr1(e):
        e.transpose(ps["A"][0:NS, 0:H], sb["ls_sb"][:, :], sb["ident_f"][0:H, 0:H])
        return e.transpose(ps["A"][0:NS, 16:16 + H], sb["lam_sb"][:, :],
                           sb["ident_f"][0:H, 0:H])
    p.step("pe", tr1, reads=["ls_sb", "lam_sb", "ident_f"], writes=["psA"], tag="p0.tr1")

    def ev1(e):
        e.copy(out=sb["lamT"][:], in_=ps["A"][0:NS, 16:16 + H])
        return e.activation(out=sb["ivs2"][:], in_=ps["A"][0:NS, 0:H],
                            func=AF.Exp, scale=-2.0)
    p.step("act", ev1, reads=["psA"], writes=["lamT", "ivs2"], tag="p0.ev1")

    def bc1(e):
        e.matmul(ps["B"][0:NS, 0:1], sb["onesrow_f"][:, 0:NS], sb["scal"][:, 0:1],
                 start=True, stop=True)
        e.matmul(ps["B"][0:NS, 1:2], sb["onesrow_f"][:, 0:NS], sb["scal"][:, 1:2],
                 start=True, stop=True)
        e.matmul(ps["B"][:, 2:3], sb["onesrow_f"][:, :], sb["scal"][:, 3:4],
                 start=True, stop=True)
        return e.matmul(ps["B"][:, 3:4], sb["onesrow_f"][:, :], sb["scal"][:, 4:5],
                        start=True, stop=True)
    p.step("pe", bc1, reads=["scal0", "scal1", "scal3", "ones"], writes=["psB"],
           tag="p0.bc1")

    def ev2(e):
        e.tensor_copy(out=sb["dsv8"][:], in_=ps["B"][0:NS, 0:1])
        e.tensor_copy(out=sb["psv8"][:], in_=ps["B"][0:NS, 1:2])
        e.tensor_copy(out=sb["bvec"][:], in_=ps["B"][:, 2:3])
        return e.tensor_copy(out=sb["ombvec"][:], in_=ps["B"][:, 3:4])
    p.step("dve", ev2, reads=["psB"], writes=["dsv8", "psv8", "bvec"], tag="p0.ev2")

    def lnb_a(e):
        e.tensor_scalar(out=sb["lnb"][:, 0:1], in0=sb["ombvec"][:, 0:1],
                        scalar1=1e-38, scalar2=None, op0=ALU.max)
        return e.tensor_scalar(out=sb["lnb"][:, 1:2], in0=sb["bvec"][:, 0:1],
                               scalar1=1e-38, scalar2=None, op0=ALU.max)
    p.step("dve", lnb_a, reads=["bvec"], writes=["lnb"], tag="p0.lnb_a")
    p.step("act", lambda e: e.activation(out=sb["lnb"][:, 0:2], in_=sb["lnb"][:, 0:2],
                                         func=AF.Ln),
           reads=["lnb"], writes=["lnb"], tag="p0.lnb_b")

    p.step("act", lambda e: e.activation(out=sb["tmp8a"][:, 0:H], in_=sb["lamT"][:],
                                         func=AF.Exp),
           reads=["lamT"], writes=["tmp8"], tag="p0.a8")
    p.step("pe", lambda e: e.matmul(ps["B"][0:1, 4:4 + H], sb["onescol8_f"][:, :],
                                    sb["tmp8a"][:, 0:H], start=True, stop=True),
           reads=["tmp8", "ones"], writes=["psB"], tag="p0.suma")

    p.step("dve", lambda e: e.tensor_scalar(out=sb["srow"][:, 5 * H:6 * H],
                                            in0=ps["B"][0:1, 4:4 + H],
                                            scalar1=sb["scal"][:, 5:6], scalar2=UBCLAMP,
                                            op0=ALU.mult, op1=ALU.min),
           reads=["psB", "scal5"], writes=["srow"], tag="p0.ubrow_a")
    p.step("dve", lambda e: e.tensor_scalar(out=sb["srow"][:, 5 * H:6 * H],
                                            in0=sb["srow"][:, 5 * H:6 * H],
                                            scalar1=-1.0, scalar2=None, op0=ALU.mult),
           reads=["srow"], writes=["srow"], tag="p0.ubrow_b")

    p.step("pe", lambda e: e.matmul(ps["B"][:, 8:8 + H], sb["onesrow_f"][:, :],
                                    sb["srow"][:, 5 * H:6 * H], start=True, stop=True),
           reads=["srow", "ones"], writes=["psB"], tag="p0.ubbc")
    p.step("dve", lambda e: e.tensor_copy(out=sb["negub"][:], in_=ps["B"][:, 8:8 + H]),
           reads=["psB"], writes=["negub"], tag="p0.negub")

    def ffn1(e):
        e.tensor_scalar(out=sb["srow"][:, 0:H], in0=sb["wstd_row"][:],
                        scalar1=1e-8, scalar2=None, op0=ALU.max)
        return e.tensor_scalar(out=sb["srow"][:, H:2 * H], in0=sb["wrec_row"][:],
                               scalar1=1e-8, scalar2=None, op0=ALU.max)
    p.step("dve", ffn1, reads=["wstd_row", "wrec_row"], writes=["srow"], tag="p0.ffn1")

    p.step("dve", lambda e: e.tensor_scalar(out=sb["srow"][:, 2 * H:3 * H],
                                            in0=sb["wrec_row"][:], scalar1=-0.1,
                                            scalar2=None, op0=ALU.add),
           reads=["wrec_row"], writes=["srow"], tag="p0.ffn2pre")

    def ffn2(e):
        e.activation(out=sb["srow"][:, 0:H], in_=sb["srow"][:, 0:H], func=AF.Sqrt)
        e.activation(out=sb["srow"][:, H:2 * H], in_=sb["srow"][:, H:2 * H], func=AF.Sqrt)
        return e.activation(out=sb["srow"][:, 2 * H:3 * H], in_=sb["srow"][:, 2 * H:3 * H],
                            func=AF.Sign)
    p.step("act", ffn2, reads=["srow", "wrec_row"], writes=["srow"], tag="p0.ffn2")

    p.step("dve", lambda e: e.tensor_scalar(out=sb["srow"][:, 2 * H:3 * H],
                                            in0=sb["srow"][:, 2 * H:3 * H],
                                            scalar1=0.0, scalar2=None, op0=ALU.max),
           reads=["srow"], writes=["srow"], tag="p0.ffn3a")

    def ffn3b(e):
        e.tensor_mul(sb["srow"][:, 3 * H:4 * H], sb["srow"][:, H:2 * H],
                     sb["srow"][:, 2 * H:3 * H])
        return e.tensor_scalar(out=sb["srow"][:, 4 * H:5 * H],
                               in0=sb["srow"][:, 2 * H:3 * H],
                               scalar1=-1.0, scalar2=1.0, op0=ALU.mult, op1=ALU.add)
    p.step("dve", ffn3b, reads=["srow"], writes=["srow"], tag="p0.ffn3b")
    p.step("dve", lambda e: e.tensor_mul(sb["srow"][:, 4 * H:5 * H],
                                         sb["srow"][:, H:2 * H],
                                         sb["srow"][:, 4 * H:5 * H]),
           reads=["srow"], writes=["srow"], tag="p0.ffn3c")

    def ffn4(e):
        e.transpose(ps["B"][0:H, 12:13], sb["srow"][:, 3 * H:4 * H],
                    sb["ident_f"][0:1, 0:1])
        return e.transpose(ps["B"][0:H, 13:14], sb["srow"][:, 4 * H:5 * H],
                           sb["ident_f"][0:1, 0:1])
    p.step("pe", ffn4, reads=["srow", "ident_f"], writes=["psB"], tag="p0.ffn4")

    def ffn5(e):
        e.tensor_copy(out=sb["col16"][:, 2:3], in_=ps["B"][0:H, 12:13])
        return e.tensor_copy(out=sb["col16"][:, 3:4], in_=ps["B"][0:H, 13:14])
    p.step("dve", ffn5, reads=["psB"], writes=["col16"], tag="p0.ffn5")

    def ffn6(e):
        for h in range(H):
            e.matmul(ps["A"][:, h:h + 1], sb["pats_sb"][:, 0:128],
                     sb["srow"][:, h:h + 1], start=True, stop=False)
            e.matmul(ps["A"][:, h:h + 1], sb["pats_sb"][:, 128:256],
                     sb["srow"][:, H + h:H + h + 1], start=False, stop=True)
        e.matmul(ps["A"][:, 16:17], sb["rep16_sb"][:, :], sb["col16"][:, 3:4],
                 start=True, stop=True)
        return e.matmul(ps["A"][:, 17:18], sb["rep16_sb"][:, :], sb["col16"][:, 2:3],
                        start=True, stop=True)
    p.step("pe", ffn6, reads=["col16", "srow", "pats_sb", "rep16_sb"], writes=["psA"],
           tag="p0.ffn6")

    def ffn7(e):
        e.mul(out=sb["qsc"][:], in_=ps["A"][:, 0:H], mul=INVSQ)
        e.copy(out=sb["ksc"][:], in_=ps["A"][:, 0:H])
        e.mul(out=sb["tailsc"][:, 0:1], in_=ps["A"][:, 16:17], mul=INVSQ)
        e.mul(out=sb["tailsc"][:, 1:2], in_=ps["A"][:, 17:18], mul=INVSQ)
        e.copy(out=sb["tailsc"][:, 2:3], in_=ps["A"][:, 16:17])
        return e.copy(out=sb["tailsc"][:, 3:4], in_=ps["A"][:, 17:18])
    p.step("act", ffn7, reads=["psA"], writes=["qsc", "ksc", "tailsc"], tag="p0.ffn7")

    for h in range(H):
        _splat_tables(env, h)


def _splat_tables(env, h):
    sb, ps, p = env["sb"], env["ps"], env["p"]
    sl = slice(h * HD, (h + 1) * HD)

    def v1a(e):
        e.tensor_scalar(out=sb["tmp8a"][:], in0=sb["pos_sb"][:, sl],
                        scalar1=sb["ivs2"][:, h:h + 1], scalar2=SQ128,
                        op0=ALU.mult, op1=ALU.mult)
        return e.tensor_mul(sb["tmp8b"][:], sb["dirs_sb"][:, sl], sb["dirs_sb"][:, sl])
    p.step("dve", v1a, reads=["pos_sb", "dirs_sb", "ivs2"], writes=["col8", "tmp8"],
           tag=f"sp{h}.v1a")

    def v1b(e):
        e.reduce_sum(out=sb["col8"][:, 0:1], in_=sb["tmp8b"][:], axis=AX.X)
        return e.tensor_mul(sb["tmp8b"][:], sb["pos_sb"][:, sl], sb["pos_sb"][:, sl])
    p.step("dve", v1b, reads=["pos_sb", "tmp8"], writes=["col8", "tmp8"], tag=f"sp{h}.v1b")

    def v1c(e):
        e.reduce_sum(out=sb["col8"][:, 1:2], in_=sb["tmp8b"][:], axis=AX.X)
        return e.tensor_mul(sb["tmp8b"][:], sb["pos_sb"][:, sl], sb["dirs_sb"][:, sl])
    p.step("dve", v1c, reads=["pos_sb", "dirs_sb", "tmp8"], writes=["col8", "tmp8"],
           tag=f"sp{h}.v1c")
    p.step("dve", lambda e: e.reduce_sum(out=sb["col8"][:, 2:3], in_=sb["tmp8b"][:],
                                         axis=AX.X),
           reads=["tmp8"], writes=["col8"], tag=f"sp{h}.v1d")

    p.step("act", lambda e: e.activation(out=sb["col8"][:, 0:1], in_=sb["col8"][:, 0:1],
                                         func=AF.Sqrt),
           reads=["col8"], writes=["col8"], tag=f"sp{h}.sqrt")

    p.step("dve", lambda e: e.tensor_scalar(out=sb["col8"][:, 0:1],
                                            in0=sb["col8"][:, 0:1],
                                            scalar1=1e-8, scalar2=None, op0=ALU.add),
           reads=["col8"], writes=["col8"], tag=f"sp{h}.v2a")
    p.step("dve", lambda e: e.reciprocal(out=sb["col8"][:, 0:1], in_=sb["col8"][:, 0:1]),
           reads=["col8"], writes=["col8"], tag=f"sp{h}.v2b")
    p.step("dve", lambda e: e.tensor_scalar(out=sb["tmp8b"][:], in0=sb["dirs_sb"][:, sl],
                                            scalar1=sb["col8"][:, 0:1], scalar2=SQ128,
                                            op0=ALU.mult, op1=ALU.mult),
           reads=["col8", "dirs_sb"], writes=["tmp8"], tag=f"sp{h}.v2c")

    def tr(e):
        e.transpose(ps["E"][:, 0:NS], sb["tmp8a"][:, :], sb["ident_f"][0:NS, 0:NS])
        return e.transpose(ps["E"][:, 32:32 + NS], sb["tmp8b"][:, :],
                           sb["ident_f"][0:NS, 0:NS])
    p.step("pe", tr, reads=["tmp8", "ident_f"], writes=["psE"], tag=f"sp{h}.tr")

    def ev(e):
        e.copy(out=sb["r1q"][:, h * 64:h * 64 + NS], in_=ps["E"][:, 0:NS])
        return e.copy(out=sb["r1q"][:, h * 64 + 32:h * 64 + 32 + NS],
                      in_=ps["E"][:, 32:32 + NS])
    p.step("act", ev, reads=["psE"], writes=["r1"], tag=f"sp{h}.ev")

    p.step("act", lambda e: e.mul(out=sb["r1k"][:, h * 64:(h + 1) * 64],
                                  in_=sb["r1q"][:, h * 64:(h + 1) * 64], mul=INVSQ),
           reads=["r1"], writes=["r1"], tag=f"sp{h}.r1k")

    def v3(e):
        e.tensor_scalar(out=sb["col8"][:, 3:4], in0=sb["ivs2"][:, h:h + 1],
                        scalar1=-64.0, scalar2=None, op0=ALU.mult)
        e.tensor_scalar(out=sb["col8"][:, 4:5], in0=sb["col8"][:, 1:2],
                        scalar1=-0.5, scalar2=sb["ivs2"][:, h:h + 1],
                        op0=ALU.mult, op1=ALU.mult)
        e.memset(sb["col8"][:, 5:6], 0.0)
        e.tensor_scalar(out=sb["col8"][:, 6:7], in0=sb["col8"][:, 2:3],
                        scalar1=sb["col8"][:, 0:1], scalar2=-1.0,
                        op0=ALU.mult, op1=ALU.mult)
        return e.tensor_scalar(out=sb["col8"][:, 7:8], in0=sb["ivs2"][:, h:h + 1],
                               scalar1=-0.5, scalar2=None, op0=ALU.mult)
    p.step("dve", v3, reads=["col8", "ivs2"], writes=["col8"], tag=f"sp{h}.v3")

    def v4(e):
        e.tensor_copy(out=sb["tmp8b"][:, 0:1], in_=sb["col8"][:, 7:8])
        return e.tensor_copy(out=sb["tmp8b"][:, 1:2], in_=sb["col8"][:, 4:5])
    p.step("dve", v4, reads=["col8"], writes=["tmp8"], tag=f"sp{h}.v4")

    def tr2(e):
        e.transpose(ps["E"][0:2, 64:64 + NS], sb["col8"][:, 3:5],
                    sb["ident_f"][0:NS, 0:NS])
        e.transpose(ps["E"][0:2, 80:80 + NS], sb["col8"][:, 5:7],
                    sb["ident_f"][0:NS, 0:NS])
        return e.transpose(ps["E"][0:2, 96:96 + NS], sb["tmp8b"][:, 0:2],
                           sb["ident_f"][0:NS, 0:NS])
    p.step("pe", tr2, reads=["col8", "tmp8", "ident_f"], writes=["psE"], tag=f"sp{h}.tr2")

    def ev2(e):
        e.copy(out=sb["rcq"][0:2, h * 64:h * 64 + NS], in_=ps["E"][0:2, 64:64 + NS])
        e.copy(out=sb["rck"][0:2, h * 64:h * 64 + NS], in_=ps["E"][0:2, 96:96 + NS])
        e.copy(out=sb["rcq"][0:2, h * 64 + 32:h * 64 + 32 + NS],
               in_=ps["E"][0:2, 80:80 + NS])
        return e.copy(out=sb["rck"][0:2, h * 64 + 32:h * 64 + 32 + NS],
                      in_=ps["E"][0:2, 80:80 + NS])
    p.step("act", ev2, reads=["psE"], writes=["rc"], tag=f"sp{h}.ev2")


def phase1_ln(env, src, use_entry=True, chunk_deps=None):
    io, sb, ps, p, bview = env["io"], env["sb"], env["ps"], env["p"], env["bview"]
    fview = env["fview"]
    hs = sb["big0"]
    entry = p.barrier_deps() if use_entry else []
    sq0, rs0 = bview(0, 2048)
    sq1, rs1 = bview(2048, 4096)
    stg, rstg = bview(4096, 4608)
    sqs = {0: (sq0, rs0), 1: (sq1, rs1)}

    sx = [ps["A"][0:1, 0:512], ps["A"][0:1, 512:1024],
          ps["B"][0:1, 0:512], ps["B"][0:1, 512:1024]]
    sxx = [ps["C"][0:1, :], ps["D"][0:1, :], ps["E"][0:1, :], ps["F"][0:1, :]]
    statres = ["psA", "psB", "psC", "psD", "psE", "psF"]

    for c in range(NCH):
        ld_deps = chunk_deps(c) if chunk_deps is not None else entry
        p.step("gp", lambda e, c=c: e.dma_start(out=hs[:, c * S:(c + 1) * S],
                                                in_=src[c * 128:(c + 1) * 128, :]),
               writes=[f"hs{c}"], deps=ld_deps, tag=f"p1.ld{c}")

    for c in range(NCH):
        par = c % 2
        sqv, sqr = sqs[par]

        def sqf(e, c=c, sqv=sqv):
            last = None
            for j in range(NT):
                last = e.tensor_mul(sqv[:, j * 512:(j + 1) * 512],
                                    hs[:, c * S + j * 512: c * S + (j + 1) * 512],
                                    hs[:, c * S + j * 512: c * S + (j + 1) * 512])
            return last
        p.step("dve", sqf, reads=[f"hs{c}"], writes=sqr, deps=entry, tag=f"p1.sq{c}")

        def stm(e, c=c, sqv=sqv):
            last = None
            for j in range(NT):
                e.matmul(sx[j], sb["onescol_b"][:, :],
                         hs[:, c * S + j * 512: c * S + (j + 1) * 512],
                         start=(c == 0), stop=(c == NCH - 1), skip_group_check=True)
                last = e.matmul(sxx[j], sb["onescol_b"][:, :], sqv[:, j * 512:(j + 1) * 512],
                                start=(c == 0), stop=(c == NCH - 1), skip_group_check=True)
            return last
        p.step("pe", stm, reads=[f"hs{c}", "ones"] + sqr, writes=statres, deps=entry,
               tag=f"p1.st{c}")

    def evrows(e):
        last = None
        for j in range(NT):
            e.copy(out=sb["row_a"][:, j * 512:(j + 1) * 512], in_=sx[j])
            last = e.copy(out=sb["row_b"][:, j * 512:(j + 1) * 512], in_=sxx[j])
        return last
    p.step("act", evrows, reads=statres, writes=["rows"], tag="p1.evr")

    def ro_a(e):
        e.tensor_scalar(out=sb["row_a"][:], in0=sb["row_a"][:], scalar1=1.0 / D,
                        scalar2=None, op0=ALU.mult)
        return e.tensor_scalar(out=sb["row_b"][:], in0=sb["row_b"][:], scalar1=1.0 / D,
                               scalar2=None, op0=ALU.mult)
    p.step("dve", ro_a, reads=["rows"], writes=["rows"], tag="p1.roa")
    p.step("dve", lambda e: e.tensor_mul(sb["row_c"][:], sb["row_a"][:], sb["row_a"][:]),
           reads=["rows"], writes=["rows"], tag="p1.rob")
    p.step("dve", lambda e: e.tensor_scalar(
        out=sb["row_b"][:], in0=sb["row_b"][:], scalar1=1e-5,
        scalar2=None, op0=ALU.add) if False else e.tensor_sub(
        sb["row_b"][:], sb["row_b"][:], sb["row_c"][:]),
           reads=["rows"], writes=["rows"], tag="p1.roc")
    p.step("dve", lambda e: e.tensor_scalar(out=sb["row_b"][:], in0=sb["row_b"][:],
                                            scalar1=1e-5, scalar2=None, op0=ALU.add),
           reads=["rows"], writes=["rows"], tag="p1.rod")
    p.step("act", lambda e: e.activation(out=sb["row_b"][:], in_=sb["row_b"][:],
                                         func=AF.Ln),
           reads=["rows"], writes=["rows"], tag="p1.rs1")
    p.step("act", lambda e: e.activation(out=sb["row_b"][:], in_=sb["row_b"][:],
                                         func=AF.Exp, scale=-0.5),
           reads=["rows"], writes=["rows"], tag="p1.rs2")

    p.step("dve", lambda e: e.tensor_mul(sb["row_c"][:], sb["row_a"][:], sb["row_b"][:]),
           reads=["rows"], writes=["rows"], tag="p1.mr")

    for j in range(NT):
        def bc(e, j=j):
            e.matmul(ps["C"][:, :], sb["onesrow_f"][:, :],
                     sb["row_c"][:, j * 512:(j + 1) * 512], start=True, stop=True)
            return e.matmul(ps["D"][:, :], sb["onesrow_f"][:, :],
                            sb["row_b"][:, j * 512:(j + 1) * 512], start=True, stop=True)
        p.step("pe", bc, reads=["rows", "ones"], writes=["psC", "psD"], tag=f"p1.bc{j}")

        def evbc(e):
            e.tensor_copy(out=sb["mrow_t"][:], in_=ps["C"][:, :])
            return e.tensor_copy(out=sb["rrow_t"][:], in_=ps["D"][:, :])
        p.step("dve", evbc, reads=["psC", "psD"],
               writes=[f"bcast{j}", "bcast"] + ([f"bcast{j-1}"] if j > 0 else []),
               tag=f"p1.eb{j}")

        for c in range(NCH):
            eng = "dve" if c % 2 == 0 else "pool"
            stgv, stgr2 = fview(c % 2 * 512, c % 2 * 512 + 512)
            sl = slice(c * S + j * 512, c * S + (j + 1) * 512)

            def norm1(e, sl=sl, stgv=stgv):
                return e.tensor_mul(stgv[:, :], hs[:, sl], sb["rrow_t"][:])
            p.step(eng, norm1, reads=[f"bcast{j}", f"hs{c}"], writes=stgr2,
                   tag=f"p1.nm{c}.{j}")

            def norm2(e, sl=sl, stgv=stgv):
                return e.tensor_sub(hs[:, sl], stgv[:, :], sb["mrow_t"][:])
            p.step(eng, norm2, reads=[f"bcast{j}"] + stgr2,
                   writes=[f"hs{c}", f"hs{c}t{j}"], tag=f"p1.nn{c}.{j}")


def phase2_qkv(env):
    io, sb, ps, p, bview = env["io"], env["sb"], env["ps"], env["p"], env["bview"]
    hs = sb["big0"]
    entry = p.barrier_deps()
    p2sids = {"qk": {}, "w16": {}, "v": {}}

    wblk = {}
    wblk["q"] = bview(10240, 18432)
    wblk["k"] = bview(18432, 26624)
    stage = {0: (bview(8192, 8704)[0], ["stg0"]), 1: (bview(8704, 9216)[0], ["stg1"])}
    sqbv = {0: (bview(9216, 9728)[0], ["sqb0"]), 1: (bview(9728, 10240)[0], ["sqb1"])}
    SSP = {0: ps["A"], 1: ps["B"]}
    QSP = {0: (ps["E"], "psE"), 1: (ps["C"], "psC")}
    SPP = {0: (ps["F"], "psF"), 1: (ps["D"], "psD")}
    w1a, w1b = sb["w16a"][0:8, 0:S], sb["w16a"][0:8, S:2 * S]
    w2a, w2b = sb["w16b"][0:8, 0:S], sb["w16b"][0:8, S:2 * S]
    feat = {0: sb["featbuf"][0:8, 0:512], 1: sb["featbuf"][0:8, 512:1024]}
    feat2 = {0: sb["featbuf"][0:8, 1024:1536], 1: sb["featbuf"][0:8, 1536:2048]}

    groups = []
    for b in range(4):
        for side in ("q", "k"):
            for h4 in range(4):
                for ti in range(NT):
                    groups.append((side, b, h4, ti))
    NG = len(groups)

    def emit_chain(gi):
        side, b, h4, ti = groups[gi]
        h = 4 * b + h4
        buf, bufr = wblk[side]
        if h4 == 0 and ti == 0:
            def emit_load(side_, b_):
                wdr = io["wqT"] if side_ == "q" else io["wkT"]
                buf2, bufr2 = wblk[side_]
                qeng = "gp" if side_ == "q" else "act"
                p.step(qeng, lambda e, wdr=wdr, buf2=buf2, b_=b_:
                       e.dma_start(out=buf2[:, :],
                                   in_=wdr[b_ * 128:(b_ + 1) * 128, :]),
                       writes=bufr2, tag=f"p2.w{side_}{b_}", dma=True)
            if side == "q" and b == 0:
                emit_load("q", 0)
            if side == "q":
                emit_load("k", b)
            elif b + 1 < 4:
                emit_load("q", b + 1)
        par = gi % 2
        pq = SSP[par]
        resp = "psA" if par == 0 else "psB"
        stg, stgr = stage[par]
        sqb, sqbr = sqbv[par]

        def mms(e, buf=buf, pq=pq, h4=h4, ti=ti):
            last = None
            for cc in range(NCH):
                c0 = cc * 512 + h4 * 128
                last = e.matmul(pq[:, 0:512], buf[:, c0:c0 + 128],
                                hs[:, cc * S + ti * 512: cc * S + (ti + 1) * 512],
                                start=(cc == 0), stop=(cc == NCH - 1))
            return last
        p.step("pe", mms, reads=bufr + [f"hs{c}t{ti}" for c in range(NCH)],
               writes=[resp], tag=f"p2.{side}{h}.mm{ti}")

    def emit_post(gi):
        side, b, h4, ti = groups[gi]
        h = 4 * b + h4
        par = gi % 2
        pq = SSP[par]
        resp = "psA" if par == 0 else "psB"
        stg, stgr = stage[par]
        sqb, sqbr = sqbv[par]
        scale = INVSQ if side == "q" else 1.0
        p.step("act", lambda e, pq=pq, stg=stg, scale=scale:
               e.mul(out=stg[:, :], in_=pq[:, 0:512], mul=scale),
               reads=[resp], writes=stgr, tag=f"p2.{side}{h}.ev{ti}")

        dst = io["q_dram"] if side == "q" else io["k_dram"]
        p2sids["qk"][(side, h, ti)] = p.step(
            "sync", lambda e, dst=dst, h=h, ti=ti, stg=stg:
            e.dma_start(out=dst[h * 128:(h + 1) * 128, ti * 512:(ti + 1) * 512],
                        in_=stg[:, :]),
            reads=stgr, tag=f"p2.{side}{h}.st{ti}")
        p.step("dve", lambda e, stg=stg, sqb=sqb:
               e.tensor_mul(sqb[:, 0:512], stg[:, :], stg[:, :]),
               reads=stgr, writes=sqbr, tag=f"p2.{side}{h}.sq{ti}")

    def emit_qs(gi):
        side, b, h4, ti = groups[gi]
        h = 4 * b + h4
        par = gi % 2
        sqb, sqbr = sqbv[par]
        qp, qpr = QSP[par]
        p.step("pe", lambda e, sqb=sqb, qp=qp:
               e.matmul(qp[0:1, 0:512], sb["onescol_b"][:, :], sqb[:, 0:512],
                        start=True, stop=True),
               reads=sqbr + ["ones"], writes=[qpr], tag=f"p2.{side}{h}.qs{ti}")
        p.step("dve", lambda e, qp=qp, par=par:
               e.tensor_copy(out=sb["qsq_pair"][0:1, par * 512:par * 512 + 512],
                             in_=qp[0:1, 0:512]),
               reads=[qpr], writes=[f"qsq{par}"], tag=f"p2.{side}{h}.qr{ti}")

    def emit_spmm(gi):
        side, b, h4, ti = groups[gi]
        h = 4 * b + h4
        par = gi % 2
        stg, stgr = stage[par]
        sp, spr = SPP[par]
        r1 = sb["r1q"] if side == "q" else sb["r1k"]
        rc = sb["rcq"] if side == "q" else sb["rck"]

        def spmm(e, r1=r1, rc=rc, h=h, stg=stg, sp=sp, par=par):
            e.matmul(sp[0:64, :], r1[:, h * 64:(h + 1) * 64],
                     stg[:, :], start=True, stop=False)
            return e.matmul(sp[0:64, :], rc[:, h * 64:(h + 1) * 64],
                            sb["qsq_pair"][:, par * 512:par * 512 + 512],
                            start=False, stop=True)
        p.step("pe", spmm, reads=["r1", "rc", f"qsq{par}"] + stgr,
               writes=[spr], tag=f"p2.{side}{h}.sp{ti}")

        ft = feat[par]
        if side == "q":
            def spev(e, h=h, ti=ti, sp=sp, ft=ft):
                e.activation(out=w1a[:, ti * 512:(ti + 1) * 512],
                             in_=sp[0:8, :], func=AF.Exp,
                             bias=sb["lamT"][:, h:h + 1])
                return e.activation(out=ft[:, :], in_=sp[32:40, :], func=AF.Tanh)
            p.step("act", spev, reads=[spr, "lamT"], writes=["w16a", f"feat{par}"],
                   tag=f"p2.q{h}.se{ti}")
            p.step("dve", lambda e, ti=ti, ft=ft:
                   e.scalar_tensor_tensor(out=w1b[:, ti * 512:(ti + 1) * 512],
                                          in0=w1a[:, ti * 512:(ti + 1) * 512],
                                          scalar=sb["dsv8"][:, 0:1],
                                          in1=ft[:, :],
                                          op0=ALU.mult, op1=ALU.mult),
                   reads=["w16a", f"feat{par}", "dsv8"], writes=["w16a"],
                   tag=f"p2.q{h}.sd{ti}")
        else:
            f2 = feat2[par]

            def spev(e, sp=sp, ft=ft, f2=f2):
                e.activation(out=f2[:, :], in_=sp[0:8, :], func=AF.Exp)
                return e.activation(out=ft[:, :], in_=sp[32:40, :], func=AF.Tanh)
            p.step("act", spev, reads=[spr], writes=[f"feat{par}", f"feat2{par}"],
                   tag=f"p2.k{h}.se{ti}")

            def spd(e, ti=ti, ft=ft, f2=f2):
                e.tensor_scalar(out=w2a[:, ti * 512:(ti + 1) * 512],
                                in0=f2[:, :], scalar1=sb["psv8"][:, 0:1],
                                scalar2=None, op0=ALU.mult)
                return e.scalar_tensor_tensor(out=w2b[:, ti * 512:(ti + 1) * 512],
                                              in0=f2[:, :],
                                              scalar=sb["psv8"][:, 0:1],
                                              in1=ft[:, :],
                                              op0=ALU.mult, op1=ALU.mult)
            p.step("dve", spd, reads=[f"feat{par}", f"feat2{par}", "psv8"],
                   writes=["w16b"], tag=f"p2.k{h}.sd{ti}")

        if ti == NT - 1:
            wnm = "w1_dram" if side == "q" else "w2_dram"
            pairs = ((w1a, 0), (w1b, 8)) if side == "q" else ((w2a, 0), (w2b, 8))
            for buf_, r0 in pairs:
                sid = p.step("sync", lambda e, wnm=wnm, buf_=buf_, r0=r0, h=h:
                             e.dma_start(out=io[wnm][h * 16 + r0:h * 16 + r0 + 8, :],
                                         in_=buf_[:, :]),
                             reads=["w16a" if side == "q" else "w16b"],
                             tag=f"p2.st{wnm}{h}.{r0}")
                p2sids["w16"].setdefault((side, h), []).append(sid)

    for gi in range(NG + 2):
        if gi < NG:
            emit_chain(gi)
        if 0 <= gi - 1 < NG:
            emit_qs(gi - 1)
        if 0 <= gi - 2 < NG:
            emit_spmm(gi - 2)
        if gi < NG:
            emit_post(gi)

    wv = {0: wblk["q"], 1: wblk["k"]}
    for fj in range(NT):
        par = fj % 2
        wvb, wvr = wv[par]
        p.step("gp", lambda e, fj=fj, wvb=wvb:
               e.dma_start(out=wvb[:, :],
                           in_=io["wvT"][fj * 128:(fj + 1) * 128, :]),
               writes=wvr, tag=f"p2.wv{fj}")
        for tc in range(NCH):
            par2 = tc % 2
            pv = ps["A"] if par2 == 0 else ps["B"]
            resp = "psA" if par2 == 0 else "psB"
            stg, stgr = stage[par2]

            def vmm(e, tc=tc, wvb=wvb, pv=pv):
                last = None
                for cc in range(NCH):
                    last = e.matmul(pv[:, 0:512],
                                    hs[:, cc * S + tc * 128: cc * S + (tc + 1) * 128],
                                    wvb[:, cc * 512:(cc + 1) * 512],
                                    start=(cc == 0), stop=(cc == NCH - 1))
                return last
            p.step("pe", vmm, reads=wvr + [f"hs{c}t{tc // 4}" for c in range(NCH)],
                   writes=[resp], tag=f"p2.v{fj}.mm{tc}")
            p.step("act", lambda e, pv=pv, stg=stg: e.copy(out=stg[:, :], in_=pv[:, 0:512]),
                   reads=[resp], writes=stgr, tag=f"p2.v{fj}.ev{tc}")
            p2sids["v"].setdefault(fj, []).append(p.step(
                "sync", lambda e, fj=fj, tc=tc, stg=stg:
                e.dma_start(out=io["v_dram"][tc * 128:(tc + 1) * 128,
                                             fj * 512:(fj + 1) * 512], in_=stg[:, :]),
                reads=stgr, tag=f"p2.v{fj}.st{tc}"))
    return p2sids


def phase4_attn(env, p2sids=None):
    io, sb, ps, p, bview, fview = (env["io"], env["sb"], env["ps"], env["p"],
                                   env["bview"], env["fview"])
    entry = p.barrier_deps()
    big0 = sb["big0"]

    qh = {0: bview(0, 2048), 1: bview(2048, 4096)}
    kh = {0: bview(4096, 6144), 1: bview(6144, 8192)}
    vh = {0: bview(8192, 10240), 1: bview(10240, 12288)}
    pout = {i: bview(12288 + 1024 * i, 13312 + 1024 * i) for i in range(4)}
    rs_sb, rsr = fview(0, 1024)
    t_sb, t_r = fview(1024, 2048)
    w1h = {0: (sb["w16a"][:, 0:S], ["w16a"]), 1: (sb["w16a"][:, S:2 * S], ["w16a2"])}
    w2h = {0: (sb["w16b"][:, 0:S], ["w16b"]), 1: (sb["w16b"][:, S:2 * S], ["w16b2"])}

    SSP = {0: ps["A"], 1: ps["B"]}
    O, L = ps["CD"], ps["EF"]
    ores = ["psC", "psD"]
    lres = ["psE", "psF"]
    deferred = []

    def flush_deferred():
        while deferred:
            deferred.pop(0)()

    for h in range(H):
        hb = h % 2
        qhb, qhr = qh[hb]
        khb, khr = kh[hb]
        vhb, vhr = vh[hb]
        w1b_, w1r = w1h[hb]
        w2b_, w2r = w2h[hb]
        early = h < 2 and p2sids is not None
        ldeng = "act" if early else "sync"
        dq = [p2sids["qk"][("q", h, ti_)] for ti_ in range(NT)] if early else entry
        dk = [p2sids["qk"][("k", h, ti_)] for ti_ in range(NT)] if early else entry
        dv = entry
        d1 = p2sids["w16"][("q", h)] if early else entry
        d2 = p2sids["w16"][("k", h)] if early else entry
        veng = "sync"
        p.step(ldeng, lambda e, h=h, qhb=qhb:
               e.dma_start(out=qhb[:, :], in_=io["q_dram"][h * 128:(h + 1) * 128, :]),
               writes=qhr, deps=dq, tag=f"p4.ldq{h}", dma=True)
        p.step(ldeng, lambda e, h=h, khb=khb:
               e.dma_start(out=khb[:, :], in_=io["k_dram"][h * 128:(h + 1) * 128, :]),
               writes=khr, deps=dk, tag=f"p4.ldk{h}", dma=True)
        p.step(veng, lambda e, h=h, vhb=vhb:
               e.dma_start(out=vhb.rearrange("p (tc f) -> p tc f", f=128),
                           in_=io["v_dram"][:, h * 128:(h + 1) * 128]
                           .rearrange("(tc p) f -> p tc f", p=128)),
               writes=vhr, deps=dv, tag=f"p4.ldv{h}", dma=True)
        p.step(ldeng, lambda e, h=h, w1b_=w1b_:
               e.dma_start(out=w1b_[:, :], in_=io["w1_dram"][h * 16:(h + 1) * 16, :]),
               writes=w1r, deps=d1, tag=f"p4.ldw1{h}", dma=True)
        p.step(ldeng, lambda e, h=h, w2b_=w2b_:
               e.dma_start(out=w2b_[:, :], in_=io["w2_dram"][h * 16:(h + 1) * 16, :]),
               writes=w2r, deps=d2, tag=f"p4.ldw2{h}", dma=True)

        for ti in range(NT):
            nj = 4 * ti + 4

            def mk_avmm(j, vhb=vhb):
                pob, por = pout[j % 4]

                def avmm(e, vhb=vhb, j=j, pob=pob,
                         first=(j == 0), last_=(j == nj - 1)):
                    vj = vhb[:, j * 128:(j + 1) * 128]
                    e.matmul(O[:, 0:512], vj, pob[:, 0:512],
                             start=first, stop=last_, skip_group_check=True)
                    e.matmul(O[:, 512:1024], vj, pob[:, 512:1024],
                             start=first, stop=last_, skip_group_check=True)
                    e.matmul(L[:, 0:512], sb["ones_b"][:, :], pob[:, 0:512],
                             start=first, stop=last_, skip_group_check=True)
                    return e.matmul(L[:, 512:1024], sb["ones_b"][:, :],
                                    pob[:, 512:1024],
                                    start=first, stop=last_, skip_group_check=True)
                p.step("pe", avmm, reads=por + vhr + ["ones"], writes=ores + lres,
                       tag=f"p4.{h}.{ti}.a{j}")

            for j in range(nj):
                par = j % 2
                sspr = "psA" if par == 0 else "psB"
                pob, por = pout[j % 4]
                diag = j >= 4 * ti

                def smm(e, qhb=qhb, khb=khb, w1b_=w1b_, w2b_=w2b_, ti=ti, j=j,
                        par=par, diag=diag):
                    S_ = SSP[par]
                    e.matmul(S_[:, 0:512], khb[:, j * 128:(j + 1) * 128],
                             qhb[:, ti * 512:(ti + 1) * 512], start=True,
                             stop=not diag, skip_group_check=True)
                    e.matmul(S_[:, 512:1024], khb[:, j * 128:(j + 1) * 128],
                             qhb[:, ti * 512:(ti + 1) * 512], start=True,
                             stop=False, skip_group_check=True)
                    last = e.matmul(S_[:, 512:1024],
                                    w2b_[:, j * 128:(j + 1) * 128],
                                    w1b_[:, ti * 512:(ti + 1) * 512],
                                    start=False, stop=not diag,
                                    skip_group_check=True)
                    if diag:
                        d = j - 4 * ti
                        e.matmul(S_[:, 0:512], sb["ident_b"][:, :],
                                 sb["masks"][:, d * 512:(d + 1) * 512],
                                 start=False, stop=True, skip_group_check=True)
                        last = e.matmul(S_[:, 512:1024], sb["ident_b"][:, :],
                                        sb["masks"][:, d * 512:(d + 1) * 512],
                                        start=False, stop=True,
                                        skip_group_check=True)
                    return last
                p.step("pe", smm, reads=qhr + khr + w1r + w2r + ["masks", "ident_b"],
                       writes=[sspr], tag=f"p4.{h}.{ti}.s{j}")

                if j >= 2:
                    mk_avmm(j - 2)

                p.step("act", lambda e, par=par, h=h, pob=pob:
                       e.activation(out=pob[:, :], in_=SSP[par][:, :], func=AF.Exp,
                                    bias=sb["negub"][:, h:h + 1]),
                       reads=[sspr, "negub"], writes=por, tag=f"p4.{h}.{ti}.e{j}")

                if j == 2:
                    flush_deferred()

            for jt in range(max(0, nj - 2), nj):
                mk_avmm(jt)

            p.step("dve", lambda e: e.tensor_copy(out=t_sb[:, :], in_=O[:, :]),
                   reads=ores, writes=t_r, tag=f"p4.{h}.{ti}.e1")
            p.step("dve", lambda e: e.tensor_copy(out=rs_sb[:, :], in_=L[:, :]),
                   reads=lres, writes=rsr, tag=f"p4.{h}.{ti}.e2")

            def mk_tail(h=h, ti=ti):
                p.step("act", lambda e: e.activation(out=rs_sb[:, :], in_=rs_sb[:, :],
                                                     func=AF.Ln),
                       reads=rsr, writes=rsr, tag=f"p4.{h}.{ti}.e3")

                def ep_r(e):
                    e.activation(out=rs_sb[:, 0:512], in_=rs_sb[:, 0:512], func=AF.Exp,
                                 scale=-1.0, bias=sb["lnb"][:, 0:1])
                    return e.activation(out=rs_sb[:, 512:1024],
                                        in_=rs_sb[:, 512:1024],
                                        func=AF.Exp, scale=-1.0,
                                        bias=sb["lnb"][:, 1:2])
                p.step("act", ep_r, reads=rsr + ["lnb"], writes=rsr,
                       tag=f"p4.{h}.{ti}.e3b")

                def ep_m(e):
                    e.tensor_mul(t_sb[:, 0:512], t_sb[:, 0:512], rs_sb[:, 0:512])
                    return e.tensor_mul(t_sb[:, 512:1024], t_sb[:, 512:1024],
                                        rs_sb[:, 512:1024])
                p.step("dve", ep_m, reads=rsr + t_r, writes=t_r,
                       tag=f"p4.{h}.{ti}.e4")
                p.step("pool", lambda e:
                       e.tensor_add(big0[:, h * S + ti * 512: h * S + (ti + 1) * 512],
                                    t_sb[:, 0:512], t_sb[:, 512:1024]),
                       reads=t_r, writes=[f"o{h}"], tag=f"p4.{h}.{ti}.o")
            deferred.append(mk_tail)
    flush_deferred()


def phase5_proj(env, w_dram, x_dram, dst, mk_b16=False):
    io, sb, ps, p, bview, fview = (env["io"], env["sb"], env["ps"], env["p"],
                                   env["bview"], env["fview"])
    src = sb["big0"]
    entry = p.barrier_deps()
    b16_sids = {}

    y = {0: bview(0, 8192)[0], 1: bview(8192, 16384)[0]}
    y_r = {(jj, ec): [f"p5y{jj}c{ec}"] for jj in range(2) for ec in range(NCH)}
    wbuf = {0: (bview(16384, 20480)[0], ["p5w0"]), 1: (bview(20480, 24576)[0], ["p5w1"])}
    sqv = {(jj, pr): (bview(24576 + 512 * (2 * jj + pr),
                            25088 + 512 * (2 * jj + pr))[0], [f"p5sq{jj}{pr}"])
           for jj in range(2) for pr in range(2)}
    xv = {0: fview(0, 512), 1: fview(512, 1024)}
    x1v = {0: fview(1024, 1536), 1: fview(1536, 2048)}
    psY = {0: ps["A"], 1: ps["B"]}
    sxs = {0: ps["E"][0:1, :], 1: ps["E"][64:65, :]}
    sxx = {0: ps["F"][0:1, :], 1: ps["F"][64:65, :]}

    def emit_block(jp, eb):
        wb, wbr = wbuf[eb % 2]
        blks = ([f"sbB{i}" for i in range(16, 20)] if eb % 2 == 0 else
                [f"sbB{i}" for i in range(20, 24)])
        p.step("gp" if eb % 2 == 0 else "sync", lambda e, eb=eb, wb=wb:
               e.dma_start(out=wb[:, :],
                           in_=w_dram[eb * 128:(eb + 1) * 128, :]),
               writes=wbr + blks + ["stg0", "stg1"],
               tag=f"p5.{jp}.w{eb}", dma=True)
        for sub in range(2):
            ec = 2 * eb + sub
            for jj in range(2):
                j = 2 * jp + jj
                pv = psY[jj]
                resp = "psA" if jj == 0 else "psB"

                def ymm(e, j=j, wb=wb, pv=pv, sub=sub):
                    last = None
                    for cc in range(NCH):
                        c0 = cc * 256 + sub * 128
                        last = e.matmul(pv[:, 0:512], wb[:, c0:c0 + 128],
                                        src[:, cc * S + j * 512: cc * S + (j + 1) * 512],
                                        start=(cc == 0), stop=(cc == NCH - 1))
                    return last
                p.step("pe", ymm, reads=wbr, writes=[resp], deps=entry,
                       tag=f"p5.{j}.mm{ec}")
            if ec >= 1:
                emit_stm(jp, ec - 1)
            for jj in range(2):
                j = 2 * jp + jj
                pv = psY[jj]
                resp = "psA" if jj == 0 else "psB"
                p.step("act", lambda e, pv=pv, ec=ec, jj=jj:
                       e.copy(out=y[jj][:, ec * 512:(ec + 1) * 512], in_=pv[:, 0:512]),
                       reads=[resp], writes=y_r[(jj, ec)], tag=f"p5.{j}.ev{ec}")
                sq, sqr = sqv[(jj, ec % 2)]
                p.step("dve", lambda e, ec=ec, jj=jj, sq=sq:
                       e.tensor_mul(sq[:, 0:512], y[jj][:, ec * 512:(ec + 1) * 512],
                                    y[jj][:, ec * 512:(ec + 1) * 512]),
                       reads=y_r[(jj, ec)], writes=sqr, tag=f"p5.{j}.sq{ec}")

    def emit_stm(jp, ec):
        for jj in range(2):
            j = 2 * jp + jj
            sq, sqr = sqv[(jj, ec % 2)]

            def stm(e, ec=ec, jj=jj, sq=sq):
                e.matmul(sxs[jj], sb["onescol_b"][:, :],
                         y[jj][:, ec * 512:(ec + 1) * 512],
                         start=(ec == 0), stop=(ec == NCH - 1), skip_group_check=True)
                return e.matmul(sxx[jj], sb["onescol_b"][:, :], sq[:, 0:512],
                                start=(ec == 0), stop=(ec == NCH - 1),
                                skip_group_check=True)
            p.step("pe", stm, reads=y_r[(jj, ec)] + sqr + ["ones"],
                   writes=["psE", "psF"], tag=f"p5.{j}.st{ec}")

    def emit_rows(jp):
        emit_stm(jp, NCH - 1)

        def evr(e):
            e.copy(out=sb["row_a"][:, 0:512], in_=sxs[0])
            e.copy(out=sb["row_a"][:, 512:1024], in_=sxs[1])
            e.copy(out=sb["row_b"][:, 0:512], in_=sxx[0])
            return e.copy(out=sb["row_b"][:, 512:1024], in_=sxx[1])
        p.step("act", evr, reads=["psE", "psF"], writes=["rows"], tag=f"p5.{jp}.er")

        def ro_a(e):
            e.tensor_scalar(out=sb["row_a"][:, 0:1024], in0=sb["row_a"][:, 0:1024],
                            scalar1=1.0 / D, scalar2=None, op0=ALU.mult)
            return e.tensor_scalar(out=sb["row_b"][:, 0:1024], in0=sb["row_b"][:, 0:1024],
                                   scalar1=1.0 / D, scalar2=None, op0=ALU.mult)
        p.step("dve", ro_a, reads=["rows"], writes=["rows"], tag=f"p5.{jp}.roa")
        p.step("dve", lambda e: e.tensor_mul(sb["row_c"][:, 0:1024],
                                             sb["row_a"][:, 0:1024],
                                             sb["row_a"][:, 0:1024]),
               reads=["rows"], writes=["rows"], tag=f"p5.{jp}.rob")
        p.step("dve", lambda e: e.tensor_sub(sb["row_b"][:, 0:1024],
                                             sb["row_b"][:, 0:1024],
                                             sb["row_c"][:, 0:1024]),
               reads=["rows"], writes=["rows"], tag=f"p5.{jp}.roc")
        p.step("dve", lambda e: e.tensor_scalar(out=sb["row_b"][:, 0:1024],
                                                in0=sb["row_b"][:, 0:1024],
                                                scalar1=1e-5, scalar2=None, op0=ALU.add),
               reads=["rows"], writes=["rows"], tag=f"p5.{jp}.rod")
        p.step("act", lambda e: e.activation(out=sb["row_b"][:, 0:1024],
                                             in_=sb["row_b"][:, 0:1024], func=AF.Ln),
               reads=["rows"], writes=["rows"], tag=f"p5.{jp}.rs1")
        p.step("act", lambda e: e.activation(out=sb["row_b"][:, 0:1024],
                                             in_=sb["row_b"][:, 0:1024], func=AF.Exp,
                                             scale=-0.5),
               reads=["rows"], writes=["rows"], tag=f"p5.{jp}.rs2")

    def emit_norm(jp, jj):
        j = 2 * jp + jj
        half = jj * 512

        def bc(e, half=half):
            e.matmul(ps["C"][:, :], sb["onesrow_f"][:, :],
                     sb["row_a"][:, half:half + 512], start=True, stop=True)
            return e.matmul(ps["D"][:, :], sb["onesrow_f"][:, :],
                            sb["row_b"][:, half:half + 512], start=True, stop=True)
        p.step("pe", bc, reads=["rows", "ones"], writes=["psC", "psD"],
               tag=f"p5.{j}.bc")

        def eb(e):
            e.tensor_copy(out=sb["mrow_t"][:], in_=ps["C"][:, :])
            return e.tensor_copy(out=sb["rrow_t"][:], in_=ps["D"][:, :])
        p.step("dve", eb, reads=["psC", "psD"], writes=["bcast"], tag=f"p5.{j}.eb")

        for ec in range(NCH):
            par2 = ec % 2
            x_sb, x_r = xv[par2]
            x1_sb, x1r = x1v[par2]
            p.step("sync", lambda e, ec=ec, j=j, x_sb=x_sb:
                   e.dma_start(out=x_sb[:, :],
                               in_=x_dram[ec * 128:(ec + 1) * 128,
                                          j * 512:(j + 1) * 512]),
                   writes=x_r, deps=entry, tag=f"p5.{j}.lx{ec}")

            neng = "dve" if ec % 2 == 0 else "pool"
            p.step(neng, lambda e, ec=ec, x1_sb=x1_sb, jj=jj: e.tensor_sub(
                x1_sb[:, :], y[jj][:, ec * 512:(ec + 1) * 512], sb["mrow_t"][:]),
                   reads=["bcast"] + y_r[(jj, ec)], writes=x1r, tag=f"p5.{j}.nm{ec}")
            p.step(neng, lambda e, x1_sb=x1_sb: e.tensor_mul(
                x1_sb[:, :], x1_sb[:, :], sb["rrow_t"][:]),
                   reads=["bcast"] + x1r, writes=x1r, tag=f"p5.{j}.nn{ec}")
            p.step(neng, lambda e, x1_sb=x1_sb, x_sb=x_sb: e.tensor_add(
                x1_sb[:, :], x1_sb[:, :], x_sb[:, :]),
                   reads=x1r + x_r, writes=x1r, tag=f"p5.{j}.na{ec}")

            p.step("act", lambda e, ec=ec, j=j, x1_sb=x1_sb:
                   e.dma_start(out=dst[ec * 128:(ec + 1) * 128,
                                       j * 512:(j + 1) * 512],
                               in_=x1_sb[:, :]),
                   reads=x1r, tag=f"p5.{j}.sx{ec}", dma=True)
            if mk_b16:
                p.step("dve", lambda e, ec=ec, jj=jj, x1_sb=x1_sb:
                       e.tensor_copy(out=y[jj][:, ec * 512:(ec + 1) * 512],
                                     in_=x1_sb[:, :]),
                       reads=x1r, writes=y_r[(jj, ec)], tag=f"p5.{j}.cb{ec}")
                sid = p.step("gp", lambda e, ec=ec, j=j, jj=jj:
                             e.dma_start(out=io["x1b_dram"][ec * 128:(ec + 1) * 128,
                                                            j * 512:(j + 1) * 512],
                                         in_=y[jj][:, ec * 512:(ec + 1) * 512]),
                             reads=y_r[(jj, ec)], tag=f"p5.{j}.sb{ec}")
                b16_sids[(j, ec)] = sid

    for eb_ in range(8):
        emit_block(0, eb_)
    emit_rows(0)
    emit_norm(0, 0)
    emit_norm(0, 1)
    for eb_ in range(8):
        emit_block(1, eb_)
    emit_rows(1)
    emit_norm(1, 0)
    emit_norm(1, 1)
    return b16_sids


def phase7_ffnproj(env):
    io, sb, ps, p, bview = env["io"], env["sb"], env["ps"], env["p"], env["bview"]
    hs = sb["big0"]
    entry = p.barrier_deps()

    wblk = {"q": bview(0, 8192), "k": bview(8192, 16384)}
    stage = {0: (bview(16384, 16896)[0], ["stg0"]), 1: (bview(16896, 17408)[0], ["stg1"])}

    for b in range(4):
        for side in ("q", "k"):
            buf, bufr = wblk[side]

            def emit_load7(side_, b_):
                wdr = io["wa_qa"] if side_ == "q" else io["wa_ka"]
                buf2, bufr2 = wblk[side_]
                qeng = "gp" if side_ == "q" else "act"
                p.step(qeng, lambda e, wdr=wdr, buf2=buf2, b_=b_:
                       e.dma_start(out=buf2[:, :],
                                   in_=wdr[b_ * 128:(b_ + 1) * 128, :]),
                       writes=bufr2, tag=f"p7.w{side_}{b_}", dma=True)
            if side == "q" and b == 0:
                emit_load7("q", 0)
            if side == "q":
                emit_load7("k", b)
            elif b + 1 < 4:
                emit_load7("q", b + 1)
            for h4 in range(4):
                h = 4 * b + h4
                for ti in range(NT):
                    par = ti % 2
                    pq = ps["A"] if par == 0 else ps["B"]
                    resp = "psA" if par == 0 else "psB"
                    stg, stgr = stage[par]

                    def mms(e, buf=buf, pq=pq, ti=ti, h4=h4):
                        last = None
                        for cc in range(NCH):
                            c0 = cc * 512 + h4 * 128
                            last = e.matmul(pq[:, 0:512], buf[:, c0:c0 + 128],
                                            hs[:, cc * S + ti * 512:
                                               cc * S + (ti + 1) * 512],
                                            start=(cc == 0), stop=(cc == NCH - 1))
                        return last
                    p.step("pe", mms,
                           reads=bufr + [f"hs{c}t{ti}" for c in range(NCH)],
                           writes=[resp], tag=f"p7.{side}{h}.mm{ti}")

                    sc = sb["qsc"] if side == "q" else sb["ksc"]
                    p.step("act", lambda e, pq=pq, stg=stg, sc=sc, h=h:
                           e.activation(out=stg[:, :], in_=pq[:, 0:512], func=AF.Copy,
                                        scale=sc[:, h:h + 1]),
                           reads=[resp, "qsc"], writes=stgr, tag=f"p7.{side}{h}.ev{ti}")

                    dstd = io["qf_dram"] if side == "q" else io["kf_dram"]
                    p.step("sync", lambda e, dstd=dstd, h=h, ti=ti, stg=stg:
                           e.dma_start(out=dstd[h * 128:h * 128 + 120,
                                                ti * 512:(ti + 1) * 512],
                                       in_=stg[0:120, :]),
                           reads=stgr, tag=f"p7.{side}{h}.st{ti}")

    wl, wlr = bview(17408, 21504)
    qlowT, qlr = bview(21504, 23552)
    klowT, klr = bview(23552, 25600)
    tails, tlr = bview(0, 4096)

    p.step("gp", lambda e:
           e.dma_start(out=wl[:, :], in_=io["wa_low"][:, :]),
           writes=wlr, deps=entry, tag="p7.wl")
    for tc in range(NCH):
        def lmm(e, tc=tc):
            last = None
            for cc in range(NCH):
                last = e.matmul(ps["E"][:, 0:256],
                                hs[:, cc * S + tc * 128: cc * S + (tc + 1) * 128],
                                wl[:, cc * 256:(cc + 1) * 256],
                                start=(cc == 0), stop=(cc == NCH - 1))
            return last
        p.step("pe", lmm, reads=wlr + [f"hs{c}t{tc // 4}" for c in range(NCH)],
               writes=["psE"], tag=f"p7.lo.mm{tc}")
        p.step("act", lambda e: e.copy(out=sb["mrow_t"][:, 0:256], in_=ps["E"][:, 0:256]),
               reads=["psE"], writes=["lowtm_f"], tag=f"p7.lo.ev{tc}")

        def ltr(e):
            e.transpose(ps["F"][:, 0:128], sb["mrow_t"][:, 0:128], sb["ident_f"][:, :])
            return e.transpose(ps["F"][:, 128:256], sb["mrow_t"][:, 128:256],
                               sb["ident_f"][:, :])
        p.step("pe", ltr, reads=["lowtm_f", "ident_f"], writes=["psF"], tag=f"p7.lo.tr{tc}")

        def lev(e, tc=tc):
            e.tensor_copy(out=qlowT[:, tc * 128:(tc + 1) * 128], in_=ps["F"][:, 0:128])
            return e.tensor_copy(out=klowT[:, tc * 128:(tc + 1) * 128],
                                 in_=ps["F"][:, 128:256])
        p.step("dve", lev, reads=["psF"], writes=qlr + klr, tag=f"p7.lo.le{tc}")

    def tail_a(e):
        e.tensor_scalar(out=tails[:, 0:2048], in0=klowT[:, :],
                        scalar1=sb["tailsc"][:, 1:2], scalar2=None, op0=ALU.mult)
        return e.tensor_scalar(out=tails[:, 2048:4096], in0=qlowT[:, :],
                               scalar1=sb["tailsc"][:, 3:4], scalar2=None, op0=ALU.mult)
    p.step("dve", tail_a, reads=qlr + klr + ["tailsc"], writes=tlr, tag="p7.tail_a")

    def tail_b(e):
        e.scalar_tensor_tensor(out=tails[:, 0:2048], in0=qlowT[:, :],
                               scalar=sb["tailsc"][:, 0:1], in1=tails[:, 0:2048],
                               op0=ALU.mult, op1=ALU.add)
        return e.scalar_tensor_tensor(out=tails[:, 2048:4096], in0=klowT[:, :],
                                      scalar=sb["tailsc"][:, 2:3], in1=tails[:, 2048:4096],
                                      op0=ALU.mult, op1=ALU.add)
    p.step("dve", tail_b, reads=qlr + klr + ["tailsc"] + tlr, writes=tlr, tag="p7.tail_b")

    for h in range(H):
        p.step("sync", lambda e, h=h:
               e.dma_start(out=io["qf_dram"][h * 128 + 120:h * 128 + 128, :],
                           in_=tails[h * 8:(h + 1) * 8, 0:2048]),
               reads=tlr, tag=f"p7.sttq{h}")
        p.step("sync", lambda e, h=h:
               e.dma_start(out=io["kf_dram"][h * 128 + 120:h * 128 + 128, :],
                           in_=tails[h * 8:(h + 1) * 8, 2048:4096]),
               reads=tlr, tag=f"p7.sttk{h}")

    wv = {0: bview(4096, 12288), 1: bview(12288, 20480)}
    stage2 = {0: (bview(25600, 26112)[0], ["v2s0"]), 1: (bview(26112, 26624)[0], ["v2s1"])}
    for fj in range(NT):
        par = fj % 2
        wvb, wvr = wv[par]
        p.step("gp", lambda e, fj=fj, wvb=wvb:
               e.dma_start(out=wvb[:, :],
                           in_=io["wa_v"][fj * 128:(fj + 1) * 128, :]),
               writes=wvr + (["stg0", "stg1"] if par == 1 else []),
               deps=entry, tag=f"p7.wv{fj}")
        for tc in range(NCH):
            par2 = tc % 2
            pv = ps["A"] if par2 == 0 else ps["B"]
            resp = "psA" if par2 == 0 else "psB"
            stg, stgr = stage2[par2]

            def vmm(e, tc=tc, wvb=wvb, pv=pv):
                last = None
                for cc in range(NCH):
                    last = e.matmul(pv[:, 0:512],
                                    hs[:, cc * S + tc * 128: cc * S + (tc + 1) * 128],
                                    wvb[:, cc * 512:(cc + 1) * 512],
                                    start=(cc == 0), stop=(cc == NCH - 1))
                return last
            p.step("pe", vmm, reads=wvr + [f"hs{c}t{tc // 4}" for c in range(NCH)],
                   writes=[resp], tag=f"p7.v{fj}.mm{tc}")
            p.step("act", lambda e, pv=pv, stg=stg: e.copy(out=stg[:, :], in_=pv[:, 0:512]),
                   reads=[resp], writes=stgr, tag=f"p7.v{fj}.ev{tc}")
            p.step("sync", lambda e, fj=fj, tc=tc, stg=stg:
                   e.dma_start(out=io["v2_dram"][tc * 128:(tc + 1) * 128,
                                                 fj * 512:(fj + 1) * 512], in_=stg[:, :]),
                   reads=stgr, tag=f"p7.v{fj}.st{tc}")


def phase8_ffnattn(env):
    io, sb, ps, p, bview, fview = (env["io"], env["sb"], env["ps"], env["p"],
                                   env["bview"], env["fview"])
    entry = p.barrier_deps()
    big0 = sb["big0"]

    qh = {0: bview(0, 2048), 1: bview(2048, 4096)}
    kh = {0: bview(4096, 6144), 1: bview(6144, 8192)}
    vh = {0: bview(8192, 10240), 1: bview(10240, 12288)}
    pout = {i: bview(12288 + 1024 * i, 13312 + 1024 * i) for i in range(4)}
    rs_sb, rsr = fview(0, 512)

    SSb = {0: ps["A"], 1: ps["B"]}
    O, L = ps["C"], ps["D"]
    deferred8 = []

    for h in range(H):
        hb = h % 2
        qhb, qhr = qh[hb]
        khb, khr = kh[hb]
        vhb, vhr = vh[hb]
        p.step("sync", lambda e, h=h, qhb=qhb:
               e.dma_start(out=qhb[:, :], in_=io["qf_dram"][h * 128:(h + 1) * 128, :]),
               writes=qhr, deps=entry, tag=f"p8.ldq{h}")
        p.step("sync", lambda e, h=h, khb=khb:
               e.dma_start(out=khb[:, :], in_=io["kf_dram"][h * 128:(h + 1) * 128, :]),
               writes=khr, deps=entry, tag=f"p8.ldk{h}")
        p.step("sync", lambda e, h=h, vhb=vhb:
               e.dma_start(out=vhb.rearrange("p (tc f) -> p tc f", f=128),
                           in_=io["v2_dram"][:, h * 128:(h + 1) * 128]
                           .rearrange("(tc p) f -> p tc f", p=128)),
               writes=vhr, deps=entry, tag=f"p8.ldv{h}")

        for ti in range(NT):
            nj = 4 * ti + 4

            def mk_avmm8(jb, vhb=vhb):
                pob, por = pout[(jb // 2) % 4]

                def avmm(e, vhb=vhb, jb=jb, pob=pob,
                         first=(jb == 0), last_=(jb == nj - 2)):
                    e.matmul(O[:, :], vhb[:, jb * 128:(jb + 1) * 128], pob[:, 0:512],
                             start=first, stop=False, skip_group_check=True)
                    e.matmul(O[:, :], vhb[:, (jb + 1) * 128:(jb + 2) * 128],
                             pob[:, 512:1024], start=False, stop=last_,
                             skip_group_check=True)
                    e.matmul(L[:, :], sb["ones_b"][:, :], pob[:, 0:512],
                             start=first, stop=False, skip_group_check=True)
                    return e.matmul(L[:, :], sb["ones_b"][:, :], pob[:, 512:1024],
                                    start=False, stop=last_, skip_group_check=True)
                p.step("pe", avmm, reads=por + vhr + ["ones"], writes=["psC", "psD"],
                       tag=f"p8.{h}.{ti}.a{jb}")

            for jb in range(0, nj, 2):
                par = (jb // 2) % 2
                sspr = "psA" if par == 0 else "psB"
                pob, por = pout[(jb // 2) % 4]

                def smm(e, qhb=qhb, khb=khb, ti=ti, jb=jb, par=par):
                    last = None
                    for jj in (jb, jb + 1):
                        half = (jj - jb) * 512
                        diag = jj >= 4 * ti
                        last = e.matmul(SSb[par][:, half:half + 512],
                                        khb[:, jj * 128:(jj + 1) * 128],
                                        qhb[:, ti * 512:(ti + 1) * 512],
                                        start=True, stop=not diag,
                                        skip_group_check=True)
                        if diag:
                            d = jj - 4 * ti
                            last = e.matmul(SSb[par][:, half:half + 512],
                                            sb["ident_b"][:, :],
                                            sb["masks"][:, d * 512:(d + 1) * 512],
                                            start=False, stop=True,
                                            skip_group_check=True)
                    return last
                p.step("pe", smm, reads=qhr + khr + ["masks", "ident_b"],
                       writes=[sspr], tag=f"p8.{h}.{ti}.s{jb}")

                if jb >= 4:
                    mk_avmm8(jb - 4)
                if jb == 2:
                    while deferred8:
                        deferred8.pop(0)()

                p.step("act", lambda e, par=par, pob=pob:
                       e.activation(out=pob[:, :], in_=SSb[par][:, :], func=AF.Exp),
                       reads=[sspr], writes=por, tag=f"p8.{h}.{ti}.e{jb}")

            if nj >= 4:
                mk_avmm8(nj - 4)
            mk_avmm8(nj - 2)

            ot_sb, otr = fview(512, 1024)
            p.step("dve", lambda e: e.tensor_copy(out=ot_sb[:, :], in_=O[:, :]),
                   reads=["psC"], writes=["psC"] + otr, tag=f"p8.{h}.{ti}.e1")
            p.step("dve", lambda e: e.tensor_copy(out=rs_sb[:, :], in_=L[:, :]),
                   reads=["psD"], writes=["psD"] + rsr, tag=f"p8.{h}.{ti}.e2")

            def mk_tail8(h=h, ti=ti):
                p.step("act", lambda e: e.activation(out=rs_sb[:, :], in_=rs_sb[:, :],
                                                     func=AF.Ln),
                       reads=rsr, writes=rsr, tag=f"p8.{h}.{ti}.e3a")
                p.step("act", lambda e: e.activation(out=rs_sb[:, :], in_=rs_sb[:, :],
                                                     func=AF.Exp, scale=-1.0),
                       reads=rsr, writes=rsr, tag=f"p8.{h}.{ti}.e3")
                p.step("pool", lambda e: e.tensor_mul(
                    big0[:, h * S + ti * 512: h * S + (ti + 1) * 512], ot_sb[:, :],
                    rs_sb[:, :]),
                       reads=otr + rsr, writes=[f"o8{h}"], tag=f"p8.{h}.{ti}.e4")
            deferred8.append(mk_tail8)
    while deferred8:
        deferred8.pop(0)()


def _prep_shared(inp):
    f32 = np.float32
    asn = lambda a: np.ascontiguousarray(np.asarray(a, dtype=f32))
    Wq, Wk, Wv, Wo = (asn(inp["L0_attn_Wq"]), asn(inp["L0_attn_Wk"]),
                      asn(inp["L0_attn_Wv"]), asn(inp["L0_attn_Wo"]))
    Wa, Wp = asn(inp["L0_ffn_Wa"]), asn(inp["L0_ffn_Wp"])
    bf = lambda a: np.ascontiguousarray(a).astype(ml_dtypes.bfloat16)

    def blk(WT, ncol):
        nb = WT.shape[1] // ncol
        out = np.empty((nb * 128, 16 * ncol), dtype=ml_dtypes.bfloat16)
        for b in range(nb):
            blkv = WT[:, b * ncol:(b + 1) * ncol].reshape(16, 128, ncol)
            out[b * 128:(b + 1) * 128] = blkv.transpose(1, 0, 2).reshape(128, -1)
        return np.ascontiguousarray(out)
    d = {
        "wqT": blk(bf(Wq.T), 512),
        "wkT": blk(bf(Wk.T), 512),
        "wvT": blk(bf(Wv.T), 512),
        "woT": blk(bf(Wo.T), 256),
        "wp": blk(bf(Wp), 256),
    }
    q_std_end = H * 120
    k_std_end = q_std_end + H * 120
    v_end = k_std_end + D
    q_low_end = v_end + H * 8
    idx_q = np.concatenate([np.concatenate([np.arange(h * 120, (h + 1) * 120),
                                            v_end + np.arange(h * 8, (h + 1) * 8)])
                            for h in range(H)])
    idx_k = np.concatenate([np.concatenate([q_std_end + np.arange(h * 120, (h + 1) * 120),
                                            q_low_end + np.arange(h * 8, (h + 1) * 8)])
                            for h in range(H)])
    d["wa_qa"] = blk(bf(Wa[:, idx_q]), 512)
    d["wa_ka"] = blk(bf(Wa[:, idx_k]), 512)
    d["wa_v"] = blk(bf(Wa[:, k_std_end:v_end]), 512)
    d["wa_low"] = blk(bf(Wa[:, v_end:]), 256)
    d["pos"] = asn(inp["L0_attn_splat_positions"])
    d["dirs"] = asn(inp["L0_attn_splat_directions"])
    d["lsc"] = asn(inp["L0_attn_splat_log_scales"])
    d["lam"] = asn(inp["L0_attn_splat_log_amplitudes"])
    d["ds"] = asn(inp["L0_attn_directional_strength"]).reshape(1, 1)
    d["ps"] = asn(inp["L0_attn_position_sensitivity"]).reshape(1, 1)
    d["gsa"] = asn(inp["L0_attn_gsa_strength"]).reshape(1, 1)
    d["wstd"] = asn(inp["L0_ffn_w_std"]).reshape(1, H)
    d["wrec"] = asn(inp["L0_ffn_w_rec"]).reshape(1, H)
    pvec = np.arange(128)
    cvec = np.arange(512)
    masks = np.zeros((4, 128, 512), dtype=f32)
    for dd in range(4):
        masks[dd] = np.where((128 * dd + pvec[:, None]) <= cvec[None, :], 0.0, NEG)
    d["masksT"] = masks.astype(ml_dtypes.bfloat16)
    d["ident"] = np.eye(128, dtype=f32)
    d["identb"] = np.eye(128).astype(ml_dtypes.bfloat16)
    pats = np.zeros((1, 256), dtype=f32)
    pats[0, 0:120] = 1.0
    pats[0, 248:256] = 1.0
    d["pats"] = pats
    rep = np.zeros((H, 128), dtype=f32)
    for n in range(H):
        rep[n, n * 8:(n + 1) * 8] = 1.0
    d["rep16"] = rep
    return d


def make_in_maps(inputs):
    x = np.asarray(inputs["L0_attn_hidden_states"], dtype=np.float32)
    shared = _prep_shared(inputs)
    in_maps = []
    for b in range(8):
        m = dict(shared)
        xt = np.ascontiguousarray(x[b].T)
        m["xT"] = xt
        m["xTb"] = xt.astype(ml_dtypes.bfloat16)
        in_maps.append(m)
    return in_maps


def kernel(**inputs):
    nc = build_program(taps=tuple(DEBUG_TAPS))
    in_maps = make_in_maps(inputs)
    res = run_bass_kernel_spmd(nc, in_maps, core_ids=list(range(8)))
    out = np.stack([np.ascontiguousarray(np.asarray(r["outT"]).T)
                    for r in res.results], axis=0)
    return out.astype(np.float32)


# version 32 (final)
# speedup vs baseline: 1.0352x; 1.0352x over previous
import contextlib
import numpy as np
import ml_dtypes
import concourse.bass as bass
import concourse.mybir as mybir
from concourse.bass_utils import run_bass_kernel_spmd

F32 = mybir.dt.float32
BF16 = mybir.dt.bfloat16
AF = mybir.ActivationFunctionType
ALU = mybir.AluOpType
AX = mybir.AxisListType

H = 16
NS = 8
S = 2048
D = 2048
HD = 128
NCH = 16
NT = 4
INVSQ = float(1.0 / np.sqrt(128.0))
SQ128 = float(np.sqrt(128.0))
NEG = -1e30
UBCLAMP = 25.0

DEBUG_TAPS = ()
DEBUG_SBUF = ()
BUILD_UPTO = 9


class Pipe:
    ENGINES = ("pe", "act", "dve", "pool", "sync", "gp")
    DMA_ENG = {"sync", "gp"}

    def __init__(self, nc, stack):
        self.nc = nc
        self.sems = {e: stack.enter_context(nc.semaphore("sem_" + e)) for e in self.ENGINES}
        self.steps = []
        self.res_w = {}
        self.res_r = {}

    def step(self, eng, fn, reads=(), writes=(), deps=(), tag="", dma=False):
        sid = len(self.steps)
        dep_set = {}

        def add_dep(d):
            if d is None:
                return
            assert d < sid
            de = self.steps[d]["eng"]
            if de == eng == "pe":
                return
            dep_set[de] = max(dep_set.get(de, -1), d)

        for d in deps:
            add_dep(d)
        for r in list(reads) + list(writes):
            for _, d in self.res_w.get(r, {}).items():
                add_dep(d)
        for r in writes:
            for _, d in self.res_r.get(r, {}).items():
                add_dep(d)
        self.steps.append({"eng": eng, "fn": fn, "deps": dep_set, "tag": tag,
                           "id": sid,
                           "inc": 16 if (eng in self.DMA_ENG or dma) else 1})
        for r in reads:
            self.res_r.setdefault(r, {})[eng] = sid
        for r in writes:
            self.res_w.setdefault(r, {})[eng] = sid
        return sid

    def barrier_deps(self):
        last = {}
        for st in self.steps:
            last[st["eng"]] = st["id"]
        return list(last.values())

    def emit(self, block):
        cnt = {e: 0 for e in self.ENGINES}
        sem_after = {}
        for st in self.steps:
            e = st["eng"]
            cnt[e] += st["inc"]
            sem_after[st["id"]] = cnt[e]

        def make_runner(names):
            def run(eng):
                watermark = {}
                for st in self.steps:
                    if st["eng"] not in names:
                        continue
                    for de, d in sorted(st["deps"].items()):
                        v = sem_after[d]
                        if watermark.get(de, 0) >= v:
                            continue
                        eng.wait_ge(self.sems[de], v)
                        watermark[de] = v
                    last = st["fn"](eng)
                    assert last is not None, f"step {st['tag']} returned None"
                    last.then_inc(self.sems[st["eng"]], st["inc"])
            return run

        for name, starter in (("pe", block.tensor), ("act", block.scalar),
                              ("dve", block.vector), ("sync", block.sync)):
            if any(st["eng"] == name for st in self.steps):
                starter(make_runner({name}))
        if any(st["eng"] in ("gp", "pool") for st in self.steps):
            block.gpsimd(make_runner({"gp", "pool"}))

        def final_wait(eng):
            for e in ("sync", "gp", "pe", "act", "dve", "pool"):
                if cnt[e]:
                    eng.wait_ge(self.sems[e], cnt[e])
        block.sync(final_wait)


def build_program(taps=()):
    nc = bass.Bass("TRN2", target_bir_lowering=False)

    def dram_in(name, shape, dtype=F32):
        return nc.dram_tensor(name, shape, dtype, kind="ExternalInput").ap()

    def dscratch(name, shape, dtype):
        kind = "ExternalOutput" if name in taps else "Internal"
        return nc.dram_tensor(name, shape, dtype, kind=kind).ap()

    io = {
        "xT": dram_in("xT", [D, S]),
        "xTb": dram_in("xTb", [D, S], BF16),
        "wqT": dram_in("wqT", [512, 8192], BF16),
        "wkT": dram_in("wkT", [512, 8192], BF16),
        "wvT": dram_in("wvT", [512, 8192], BF16),
        "woT": dram_in("woT", [1024, 4096], BF16),
        "wa_qa": dram_in("wa_qa", [512, 8192], BF16),
        "wa_ka": dram_in("wa_ka", [512, 8192], BF16),
        "wa_v": dram_in("wa_v", [512, 8192], BF16),
        "wa_low": dram_in("wa_low", [128, 4096], BF16),
        "wp": dram_in("wp", [1024, 4096], BF16),
        "pos": dram_in("pos", [H, NS, HD]), "dirs": dram_in("dirs", [H, NS, HD]),
        "lsc": dram_in("lsc", [H, NS]), "lam": dram_in("lam", [H, NS]),
        "ds": dram_in("ds", [1, 1]), "ps": dram_in("ps", [1, 1]),
        "gsa": dram_in("gsa", [1, 1]),
        "wstd": dram_in("wstd", [1, H]), "wrec": dram_in("wrec", [1, H]),
        "masksT": dram_in("masksT", [4, 128, 512], BF16),
        "ident": dram_in("ident", [128, 128]),
        "identb": dram_in("identb", [128, 128], BF16),
        "pats": dram_in("pats", [1, 256]),
        "rep16": dram_in("rep16", [H, 128]),
        "outT": nc.dram_tensor("outT", [D, S], F32, kind="ExternalOutput").ap(),
        "q_dram": dscratch("q_dram", [D, S], BF16),
        "k_dram": dscratch("k_dram", [D, S], BF16),
        "v_dram": dscratch("v_dram", [S, D], BF16),
        "w1_dram": dscratch("w1_dram", [H * 16, S], BF16),
        "w2_dram": dscratch("w2_dram", [H * 16, S], BF16),
        "x1_dram": dscratch("x1_dram", [D, S], F32),
        "x1b_dram": dscratch("x1b_dram", [D, S], BF16),
        "qf_dram": dscratch("qf_dram", [D, S], BF16),
        "kf_dram": dscratch("kf_dram", [D, S], BF16),
        "v2_dram": dscratch("v2_dram", [S, D], BF16),
    }

    with contextlib.ExitStack() as stack:
        sb = {}

        def alloc(name, shape, dtype):
            sb[name] = stack.enter_context(nc.sbuf_tensor(name, shape, dtype))

        alloc("masks", [128, 4 * 512], BF16)
        alloc("ident_f", [128, 128], F32)
        alloc("ident_b", [128, 128], BF16)
        alloc("ones_b", [128, 128], BF16)
        alloc("onescol_b", [128, 1], BF16)
        alloc("onesrow_f", [1, 128], F32)
        alloc("onescol8_f", [NS, 1], F32)
        alloc("pos_sb", [NS, H * HD], F32)
        alloc("dirs_sb", [NS, H * HD], F32)
        alloc("ls_sb", [H, NS], F32)
        alloc("lam_sb", [H, NS], F32)
        alloc("lamT", [NS, H], F32)
        alloc("ivs2", [NS, H], F32)
        alloc("scal", [1, 8], F32)
        alloc("dsv8", [NS, 1], F32)
        alloc("psv8", [NS, 1], F32)
        alloc("bvec", [128, 1], F32)
        alloc("ombvec", [128, 1], F32)
        alloc("lnb", [128, 2], F32)
        alloc("negub", [128, H], F32)
        alloc("wstd_row", [1, H], F32)
        alloc("wrec_row", [1, H], F32)
        alloc("srow", [1, 8 * H], F32)
        alloc("qsc", [128, H], F32)
        alloc("ksc", [128, H], F32)
        alloc("tailsc", [128, 4], F32)
        alloc("pats_sb", [1, 256], F32)
        alloc("rep16_sb", [H, 128], F32)
        alloc("col16", [H, 4], F32)
        alloc("r1q", [128, H * 64], BF16)
        alloc("r1k", [128, H * 64], BF16)
        alloc("rcq", [2, H * 64], BF16)
        alloc("rck", [2, H * 64], BF16)
        alloc("qsq_pair", [2, 1024], BF16)
        alloc("row_a", [1, 2048], F32)
        alloc("row_b", [1, 2048], F32)
        alloc("row_c", [1, 2048], F32)
        alloc("tmp8a", [NS, HD], F32)
        alloc("tmp8b", [NS, HD], F32)
        alloc("col8", [NS, 8], F32)
        alloc("mrow_t", [128, 512], F32)
        alloc("rrow_t", [128, 512], F32)
        alloc("big0", [128, H * S], BF16)
        alloc("scr_b", [128, 26 * 1024], BF16)
        alloc("scr_f", [128, 4 * 512], F32)
        alloc("w16a", [16, 2 * S], BF16)
        alloc("w16b", [16, 2 * S], BF16)
        alloc("featbuf", [8, 4 * 512], BF16)

        ps = {}
        for nm, shape, dt in (("A", [128, 1024], F32), ("B", [128, 1024], F32),
                              ("CD", [128, 1024], F32), ("EF", [128, 1024], F32)):
            ps[nm] = stack.enter_context(nc.psum_tensor("ps" + nm, shape, dt))
        ps["C"] = ps["CD"][:, 0:512]
        ps["D"] = ps["CD"][:, 512:1024]
        ps["E"] = ps["EF"][:, 0:512]
        ps["F"] = ps["EF"][:, 512:1024]

        pipe = Pipe(nc, stack)
        env = {"nc": nc, "io": io, "sb": sb, "ps": ps, "p": pipe}

        def bview(c0, c1):
            assert 0 <= c0 < c1 <= 26 * 1024
            return sb["scr_b"][:, c0:c1], [f"sbB{i}" for i in
                                           range(c0 // 1024, (c1 + 1023) // 1024)]

        def fview(c0, c1):
            assert 0 <= c0 < c1 <= 6 * 512
            return sb["scr_f"][:, c0:c1], [f"sfB{i}" for i in
                                           range(c0 // 512, (c1 + 511) // 512)]
        env["bview"] = bview
        env["fview"] = fview

        phase0(env)
        if BUILD_UPTO >= 1:
            phase1_ln(env, src=io["xTb"], use_entry=False)
        p2sids = None
        if BUILD_UPTO >= 2:
            p2sids = phase2_qkv(env)
        if BUILD_UPTO >= 4:
            phase4_attn(env, p2sids)
        if BUILD_UPTO >= 5:
            b16_sids = phase5_proj(env, w_dram=io["woT"], x_dram=io["xT"],
                                   dst=io["x1_dram"], mk_b16=True)
        if BUILD_UPTO >= 6:
            phase1_ln(env, src=io["x1b_dram"],
                      chunk_deps=lambda c: [b16_sids[(j, c)] for j in range(NT)])
        if BUILD_UPTO >= 7:
            phase7_ffnproj(env)
        if BUILD_UPTO >= 8:
            phase8_ffnattn(env)
        if BUILD_UPTO >= 9:
            phase5_proj(env, w_dram=io["wp"], x_dram=io["x1_dram"], dst=io["outT"])

        fin = pipe.barrier_deps()
        for nm in DEBUG_SBUF:
            t = sb[nm]
            shape = [t.shape[0], t.shape[1]]
            dt = t.dtype
            dbg = nc.dram_tensor("dbg_" + nm, shape, dt, kind="ExternalOutput").ap()
            pipe.step("sync", lambda e, t=t, dbg=dbg: e.dma_start(out=dbg, in_=t[:, :]),
                      deps=fin, tag="dbg." + nm)

        with nc.Block() as block:
            pipe.emit(block)

    return nc


def phase0(env):
    io, sb, ps, p = env["io"], env["sb"], env["ps"], env["p"]

    def ld(name, out, in_):
        p.step("sync", lambda e, o=out, i=in_: e.dma_start(out=o, in_=i),
               writes=[name], tag="p0.ld." + name)

    ld("masks", sb["masks"][:, :].rearrange("p (d n) -> p d n", n=512),
       io["masksT"].rearrange("d p n -> p d n"))
    ld("ident_f", sb["ident_f"][:], io["ident"])
    ld("ident_b", sb["ident_b"][:], io["identb"])
    ld("pos_sb", sb["pos_sb"][:, :].rearrange("n (h d) -> n h d", d=HD),
       io["pos"].rearrange("h n d -> n h d"))
    ld("dirs_sb", sb["dirs_sb"][:, :].rearrange("n (h d) -> n h d", d=HD),
       io["dirs"].rearrange("h n d -> n h d"))
    ld("ls_sb", sb["ls_sb"][:], io["lsc"])
    ld("lam_sb", sb["lam_sb"][:], io["lam"])
    ld("wstd_row", sb["wstd_row"][:], io["wstd"])
    ld("wrec_row", sb["wrec_row"][:], io["wrec"])
    ld("pats_sb", sb["pats_sb"][:], io["pats"])
    ld("rep16_sb", sb["rep16_sb"][:], io["rep16"])
    ld("scal0", sb["scal"][:, 0:1], io["ds"])
    ld("scal1", sb["scal"][:, 1:2], io["ps"])
    ld("scal2", sb["scal"][:, 2:3], io["gsa"])

    def msets(e):
        e.memset(sb["ones_b"][:], 1.0)
        e.memset(sb["onescol_b"][:], 1.0)
        e.memset(sb["onesrow_f"][:], 1.0)
        e.memset(sb["onescol8_f"][:], 1.0)
        e.memset(sb["qsq_pair"][:], 1.0)
        e.memset(sb["r1q"][:], 0.0)
        e.memset(sb["r1k"][:], 0.0)
        e.memset(sb["rcq"][:], 0.0)
        return e.memset(sb["rck"][:], 0.0)
    p.step("dve", msets, writes=["ones", "r1", "rc", "qsq_pair"], tag="p0.memset")

    p.step("act", lambda e: e.activation(out=sb["scal"][:, 3:4], in_=sb["scal"][:, 2:3],
                                         func=AF.Sigmoid),
           reads=["scal2"], writes=["scal3"], tag="p0.sig")

    def blend_a(e):
        e.tensor_scalar(out=sb["scal"][:, 3:4], in0=sb["scal"][:, 3:4],
                        scalar1=0.15, scalar2=0.03, op0=ALU.mult, op1=ALU.min)
        e.tensor_reduce(out=sb["scal"][:, 5:6], in_=sb["scal"][:, 1:2], axis=AX.X,
                        op=ALU.max, apply_absolute_value=True)
        return e.tensor_reduce(out=sb["scal"][:, 6:7], in_=sb["scal"][:, 0:1], axis=AX.X,
                               op=ALU.max, apply_absolute_value=True)
    p.step("dve", blend_a, reads=["scal3", "scal0", "scal1"], writes=["scal3", "scal5"],
           tag="p0.blend_a")

    def blend_b(e):
        e.tensor_scalar(out=sb["scal"][:, 4:5], in0=sb["scal"][:, 3:4],
                        scalar1=-1.0, scalar2=1.0, op0=ALU.mult, op1=ALU.add)
        return e.tensor_scalar(out=sb["scal"][:, 6:7], in0=sb["scal"][:, 6:7],
                               scalar1=1.0, scalar2=None, op0=ALU.add)
    p.step("dve", blend_b, reads=["scal3"], writes=["scal3", "scal5"], tag="p0.blend_b")
    p.step("dve", lambda e: e.tensor_mul(sb["scal"][:, 5:6], sb["scal"][:, 5:6],
                                         sb["scal"][:, 6:7]),
           reads=["scal3"], writes=["scal5"], tag="p0.blend_c")

    def tr1(e):
        e.transpose(ps["A"][0:NS, 0:H], sb["ls_sb"][:, :], sb["ident_f"][0:H, 0:H])
        return e.transpose(ps["A"][0:NS, 16:16 + H], sb["lam_sb"][:, :],
                           sb["ident_f"][0:H, 0:H])
    p.step("pe", tr1, reads=["ls_sb", "lam_sb", "ident_f"], writes=["psA"], tag="p0.tr1")

    def ev1(e):
        e.copy(out=sb["lamT"][:], in_=ps["A"][0:NS, 16:16 + H])
        return e.activation(out=sb["ivs2"][:], in_=ps["A"][0:NS, 0:H],
                            func=AF.Exp, scale=-2.0)
    p.step("act", ev1, reads=["psA"], writes=["lamT", "ivs2"], tag="p0.ev1")

    def bc1(e):
        e.matmul(ps["B"][0:NS, 0:1], sb["onesrow_f"][:, 0:NS], sb["scal"][:, 0:1],
                 start=True, stop=True)
        e.matmul(ps["B"][0:NS, 1:2], sb["onesrow_f"][:, 0:NS], sb["scal"][:, 1:2],
                 start=True, stop=True)
        e.matmul(ps["B"][:, 2:3], sb["onesrow_f"][:, :], sb["scal"][:, 3:4],
                 start=True, stop=True)
        return e.matmul(ps["B"][:, 3:4], sb["onesrow_f"][:, :], sb["scal"][:, 4:5],
                        start=True, stop=True)
    p.step("pe", bc1, reads=["scal0", "scal1", "scal3", "ones"], writes=["psB"],
           tag="p0.bc1")

    def ev2(e):
        e.tensor_copy(out=sb["dsv8"][:], in_=ps["B"][0:NS, 0:1])
        e.tensor_copy(out=sb["psv8"][:], in_=ps["B"][0:NS, 1:2])
        e.tensor_copy(out=sb["bvec"][:], in_=ps["B"][:, 2:3])
        return e.tensor_copy(out=sb["ombvec"][:], in_=ps["B"][:, 3:4])
    p.step("dve", ev2, reads=["psB"], writes=["dsv8", "psv8", "bvec"], tag="p0.ev2")

    def lnb_a(e):
        e.tensor_scalar(out=sb["lnb"][:, 0:1], in0=sb["ombvec"][:, 0:1],
                        scalar1=1e-38, scalar2=None, op0=ALU.max)
        return e.tensor_scalar(out=sb["lnb"][:, 1:2], in0=sb["bvec"][:, 0:1],
                               scalar1=1e-38, scalar2=None, op0=ALU.max)
    p.step("dve", lnb_a, reads=["bvec"], writes=["lnb"], tag="p0.lnb_a")
    p.step("act", lambda e: e.activation(out=sb["lnb"][:, 0:2], in_=sb["lnb"][:, 0:2],
                                         func=AF.Ln),
           reads=["lnb"], writes=["lnb"], tag="p0.lnb_b")

    p.step("act", lambda e: e.activation(out=sb["tmp8a"][:, 0:H], in_=sb["lamT"][:],
                                         func=AF.Exp),
           reads=["lamT"], writes=["tmp8"], tag="p0.a8")
    p.step("pe", lambda e: e.matmul(ps["B"][0:1, 4:4 + H], sb["onescol8_f"][:, :],
                                    sb["tmp8a"][:, 0:H], start=True, stop=True),
           reads=["tmp8", "ones"], writes=["psB"], tag="p0.suma")

    p.step("dve", lambda e: e.tensor_scalar(out=sb["srow"][:, 5 * H:6 * H],
                                            in0=ps["B"][0:1, 4:4 + H],
                                            scalar1=sb["scal"][:, 5:6], scalar2=UBCLAMP,
                                            op0=ALU.mult, op1=ALU.min),
           reads=["psB", "scal5"], writes=["srow"], tag="p0.ubrow_a")
    p.step("dve", lambda e: e.tensor_scalar(out=sb["srow"][:, 5 * H:6 * H],
                                            in0=sb["srow"][:, 5 * H:6 * H],
                                            scalar1=-1.0, scalar2=None, op0=ALU.mult),
           reads=["srow"], writes=["srow"], tag="p0.ubrow_b")

    p.step("pe", lambda e: e.matmul(ps["B"][:, 8:8 + H], sb["onesrow_f"][:, :],
                                    sb["srow"][:, 5 * H:6 * H], start=True, stop=True),
           reads=["srow", "ones"], writes=["psB"], tag="p0.ubbc")
    p.step("dve", lambda e: e.tensor_copy(out=sb["negub"][:], in_=ps["B"][:, 8:8 + H]),
           reads=["psB"], writes=["negub"], tag="p0.negub")

    def ffn1(e):
        e.tensor_scalar(out=sb["srow"][:, 0:H], in0=sb["wstd_row"][:],
                        scalar1=1e-8, scalar2=None, op0=ALU.max)
        return e.tensor_scalar(out=sb["srow"][:, H:2 * H], in0=sb["wrec_row"][:],
                               scalar1=1e-8, scalar2=None, op0=ALU.max)
    p.step("dve", ffn1, reads=["wstd_row", "wrec_row"], writes=["srow"], tag="p0.ffn1")

    p.step("dve", lambda e: e.tensor_scalar(out=sb["srow"][:, 2 * H:3 * H],
                                            in0=sb["wrec_row"][:], scalar1=-0.1,
                                            scalar2=None, op0=ALU.add),
           reads=["wrec_row"], writes=["srow"], tag="p0.ffn2pre")

    def ffn2(e):
        e.activation(out=sb["srow"][:, 0:H], in_=sb["srow"][:, 0:H], func=AF.Sqrt)
        e.activation(out=sb["srow"][:, H:2 * H], in_=sb["srow"][:, H:2 * H], func=AF.Sqrt)
        return e.activation(out=sb["srow"][:, 2 * H:3 * H], in_=sb["srow"][:, 2 * H:3 * H],
                            func=AF.Sign)
    p.step("act", ffn2, reads=["srow", "wrec_row"], writes=["srow"], tag="p0.ffn2")

    p.step("dve", lambda e: e.tensor_scalar(out=sb["srow"][:, 2 * H:3 * H],
                                            in0=sb["srow"][:, 2 * H:3 * H],
                                            scalar1=0.0, scalar2=None, op0=ALU.max),
           reads=["srow"], writes=["srow"], tag="p0.ffn3a")

    def ffn3b(e):
        e.tensor_mul(sb["srow"][:, 3 * H:4 * H], sb["srow"][:, H:2 * H],
                     sb["srow"][:, 2 * H:3 * H])
        return e.tensor_scalar(out=sb["srow"][:, 4 * H:5 * H],
                               in0=sb["srow"][:, 2 * H:3 * H],
                               scalar1=-1.0, scalar2=1.0, op0=ALU.mult, op1=ALU.add)
    p.step("dve", ffn3b, reads=["srow"], writes=["srow"], tag="p0.ffn3b")
    p.step("dve", lambda e: e.tensor_mul(sb["srow"][:, 4 * H:5 * H],
                                         sb["srow"][:, H:2 * H],
                                         sb["srow"][:, 4 * H:5 * H]),
           reads=["srow"], writes=["srow"], tag="p0.ffn3c")

    def ffn4(e):
        e.transpose(ps["B"][0:H, 12:13], sb["srow"][:, 3 * H:4 * H],
                    sb["ident_f"][0:1, 0:1])
        return e.transpose(ps["B"][0:H, 13:14], sb["srow"][:, 4 * H:5 * H],
                           sb["ident_f"][0:1, 0:1])
    p.step("pe", ffn4, reads=["srow", "ident_f"], writes=["psB"], tag="p0.ffn4")

    def ffn5(e):
        e.tensor_copy(out=sb["col16"][:, 2:3], in_=ps["B"][0:H, 12:13])
        return e.tensor_copy(out=sb["col16"][:, 3:4], in_=ps["B"][0:H, 13:14])
    p.step("dve", ffn5, reads=["psB"], writes=["col16"], tag="p0.ffn5")

    def ffn6(e):
        for h in range(H):
            e.matmul(ps["A"][:, h:h + 1], sb["pats_sb"][:, 0:128],
                     sb["srow"][:, h:h + 1], start=True, stop=False)
            e.matmul(ps["A"][:, h:h + 1], sb["pats_sb"][:, 128:256],
                     sb["srow"][:, H + h:H + h + 1], start=False, stop=True)
        e.matmul(ps["A"][:, 16:17], sb["rep16_sb"][:, :], sb["col16"][:, 3:4],
                 start=True, stop=True)
        return e.matmul(ps["A"][:, 17:18], sb["rep16_sb"][:, :], sb["col16"][:, 2:3],
                        start=True, stop=True)
    p.step("pe", ffn6, reads=["col16", "srow", "pats_sb", "rep16_sb"], writes=["psA"],
           tag="p0.ffn6")

    def ffn7(e):
        e.mul(out=sb["qsc"][:], in_=ps["A"][:, 0:H], mul=INVSQ)
        e.copy(out=sb["ksc"][:], in_=ps["A"][:, 0:H])
        e.mul(out=sb["tailsc"][:, 0:1], in_=ps["A"][:, 16:17], mul=INVSQ)
        e.mul(out=sb["tailsc"][:, 1:2], in_=ps["A"][:, 17:18], mul=INVSQ)
        e.copy(out=sb["tailsc"][:, 2:3], in_=ps["A"][:, 16:17])
        return e.copy(out=sb["tailsc"][:, 3:4], in_=ps["A"][:, 17:18])
    p.step("act", ffn7, reads=["psA"], writes=["qsc", "ksc", "tailsc"], tag="p0.ffn7")

    for h in range(H):
        _splat_tables(env, h)


def _splat_tables(env, h):
    sb, ps, p = env["sb"], env["ps"], env["p"]
    sl = slice(h * HD, (h + 1) * HD)

    def v1a(e):
        e.tensor_scalar(out=sb["tmp8a"][:], in0=sb["pos_sb"][:, sl],
                        scalar1=sb["ivs2"][:, h:h + 1], scalar2=SQ128,
                        op0=ALU.mult, op1=ALU.mult)
        return e.tensor_mul(sb["tmp8b"][:], sb["dirs_sb"][:, sl], sb["dirs_sb"][:, sl])
    p.step("dve", v1a, reads=["pos_sb", "dirs_sb", "ivs2"], writes=["col8", "tmp8"],
           tag=f"sp{h}.v1a")

    def v1b(e):
        e.reduce_sum(out=sb["col8"][:, 0:1], in_=sb["tmp8b"][:], axis=AX.X)
        return e.tensor_mul(sb["tmp8b"][:], sb["pos_sb"][:, sl], sb["pos_sb"][:, sl])
    p.step("dve", v1b, reads=["pos_sb", "tmp8"], writes=["col8", "tmp8"], tag=f"sp{h}.v1b")

    def v1c(e):
        e.reduce_sum(out=sb["col8"][:, 1:2], in_=sb["tmp8b"][:], axis=AX.X)
        return e.tensor_mul(sb["tmp8b"][:], sb["pos_sb"][:, sl], sb["dirs_sb"][:, sl])
    p.step("dve", v1c, reads=["pos_sb", "dirs_sb", "tmp8"], writes=["col8", "tmp8"],
           tag=f"sp{h}.v1c")
    p.step("dve", lambda e: e.reduce_sum(out=sb["col8"][:, 2:3], in_=sb["tmp8b"][:],
                                         axis=AX.X),
           reads=["tmp8"], writes=["col8"], tag=f"sp{h}.v1d")

    p.step("act", lambda e: e.activation(out=sb["col8"][:, 0:1], in_=sb["col8"][:, 0:1],
                                         func=AF.Sqrt),
           reads=["col8"], writes=["col8"], tag=f"sp{h}.sqrt")

    p.step("dve", lambda e: e.tensor_scalar(out=sb["col8"][:, 0:1],
                                            in0=sb["col8"][:, 0:1],
                                            scalar1=1e-8, scalar2=None, op0=ALU.add),
           reads=["col8"], writes=["col8"], tag=f"sp{h}.v2a")
    p.step("dve", lambda e: e.reciprocal(out=sb["col8"][:, 0:1], in_=sb["col8"][:, 0:1]),
           reads=["col8"], writes=["col8"], tag=f"sp{h}.v2b")
    p.step("dve", lambda e: e.tensor_scalar(out=sb["tmp8b"][:], in0=sb["dirs_sb"][:, sl],
                                            scalar1=sb["col8"][:, 0:1], scalar2=SQ128,
                                            op0=ALU.mult, op1=ALU.mult),
           reads=["col8", "dirs_sb"], writes=["tmp8"], tag=f"sp{h}.v2c")

    def tr(e):
        e.transpose(ps["E"][:, 0:NS], sb["tmp8a"][:, :], sb["ident_f"][0:NS, 0:NS])
        return e.transpose(ps["E"][:, 32:32 + NS], sb["tmp8b"][:, :],
                           sb["ident_f"][0:NS, 0:NS])
    p.step("pe", tr, reads=["tmp8", "ident_f"], writes=["psE"], tag=f"sp{h}.tr")

    def ev(e):
        e.copy(out=sb["r1q"][:, h * 64:h * 64 + NS], in_=ps["E"][:, 0:NS])
        return e.copy(out=sb["r1q"][:, h * 64 + 32:h * 64 + 32 + NS],
                      in_=ps["E"][:, 32:32 + NS])
    p.step("act", ev, reads=["psE"], writes=["r1"], tag=f"sp{h}.ev")

    p.step("act", lambda e: e.mul(out=sb["r1k"][:, h * 64:(h + 1) * 64],
                                  in_=sb["r1q"][:, h * 64:(h + 1) * 64], mul=INVSQ),
           reads=["r1"], writes=["r1"], tag=f"sp{h}.r1k")

    def v3(e):
        e.tensor_scalar(out=sb["col8"][:, 3:4], in0=sb["ivs2"][:, h:h + 1],
                        scalar1=-64.0, scalar2=None, op0=ALU.mult)
        e.tensor_scalar(out=sb["col8"][:, 4:5], in0=sb["col8"][:, 1:2],
                        scalar1=-0.5, scalar2=sb["ivs2"][:, h:h + 1],
                        op0=ALU.mult, op1=ALU.mult)
        e.memset(sb["col8"][:, 5:6], 0.0)
        e.tensor_scalar(out=sb["col8"][:, 6:7], in0=sb["col8"][:, 2:3],
                        scalar1=sb["col8"][:, 0:1], scalar2=-1.0,
                        op0=ALU.mult, op1=ALU.mult)
        return e.tensor_scalar(out=sb["col8"][:, 7:8], in0=sb["ivs2"][:, h:h + 1],
                               scalar1=-0.5, scalar2=None, op0=ALU.mult)
    p.step("dve", v3, reads=["col8", "ivs2"], writes=["col8"], tag=f"sp{h}.v3")

    def v4(e):
        e.tensor_copy(out=sb["tmp8b"][:, 0:1], in_=sb["col8"][:, 7:8])
        return e.tensor_copy(out=sb["tmp8b"][:, 1:2], in_=sb["col8"][:, 4:5])
    p.step("dve", v4, reads=["col8"], writes=["tmp8"], tag=f"sp{h}.v4")

    def tr2(e):
        e.transpose(ps["E"][0:2, 64:64 + NS], sb["col8"][:, 3:5],
                    sb["ident_f"][0:NS, 0:NS])
        e.transpose(ps["E"][0:2, 80:80 + NS], sb["col8"][:, 5:7],
                    sb["ident_f"][0:NS, 0:NS])
        return e.transpose(ps["E"][0:2, 96:96 + NS], sb["tmp8b"][:, 0:2],
                           sb["ident_f"][0:NS, 0:NS])
    p.step("pe", tr2, reads=["col8", "tmp8", "ident_f"], writes=["psE"], tag=f"sp{h}.tr2")

    def ev2(e):
        e.copy(out=sb["rcq"][0:2, h * 64:h * 64 + NS], in_=ps["E"][0:2, 64:64 + NS])
        e.copy(out=sb["rck"][0:2, h * 64:h * 64 + NS], in_=ps["E"][0:2, 96:96 + NS])
        e.copy(out=sb["rcq"][0:2, h * 64 + 32:h * 64 + 32 + NS],
               in_=ps["E"][0:2, 80:80 + NS])
        return e.copy(out=sb["rck"][0:2, h * 64 + 32:h * 64 + 32 + NS],
                      in_=ps["E"][0:2, 80:80 + NS])
    p.step("act", ev2, reads=["psE"], writes=["rc"], tag=f"sp{h}.ev2")


def phase1_ln(env, src, use_entry=True, chunk_deps=None):
    io, sb, ps, p, bview = env["io"], env["sb"], env["ps"], env["p"], env["bview"]
    fview = env["fview"]
    hs = sb["big0"]
    entry = p.barrier_deps() if use_entry else []
    sq0, rs0 = bview(0, 2048)
    sq1, rs1 = bview(2048, 4096)
    stg, rstg = bview(4096, 4608)
    sqs = {0: (sq0, rs0), 1: (sq1, rs1)}

    sx = [ps["A"][0:1, 0:512], ps["A"][0:1, 512:1024],
          ps["B"][0:1, 0:512], ps["B"][0:1, 512:1024]]
    sxx = [ps["C"][0:1, :], ps["D"][0:1, :], ps["E"][0:1, :], ps["F"][0:1, :]]
    statres = ["psA", "psB", "psC", "psD", "psE", "psF"]

    for c in range(NCH):
        ld_deps = chunk_deps(c) if chunk_deps is not None else entry
        p.step("gp" if c % 2 == 0 else "act",
               lambda e, c=c: e.dma_start(out=hs[:, c * S:(c + 1) * S],
                                          in_=src[c * 128:(c + 1) * 128, :]),
               writes=[f"hs{c}"], deps=ld_deps, tag=f"p1.ld{c}", dma=True)

    for c in range(NCH):
        par = c % 2
        sqv, sqr = sqs[par]

        def sqf(e, c=c, sqv=sqv):
            last = None
            for j in range(NT):
                last = e.tensor_mul(sqv[:, j * 512:(j + 1) * 512],
                                    hs[:, c * S + j * 512: c * S + (j + 1) * 512],
                                    hs[:, c * S + j * 512: c * S + (j + 1) * 512])
            return last
        p.step("dve", sqf, reads=[f"hs{c}"], writes=sqr, deps=entry, tag=f"p1.sq{c}")

        def stm(e, c=c, sqv=sqv):
            last = None
            for j in range(NT):
                e.matmul(sx[j], sb["onescol_b"][:, :],
                         hs[:, c * S + j * 512: c * S + (j + 1) * 512],
                         start=(c == 0), stop=(c == NCH - 1), skip_group_check=True)
                last = e.matmul(sxx[j], sb["onescol_b"][:, :], sqv[:, j * 512:(j + 1) * 512],
                                start=(c == 0), stop=(c == NCH - 1), skip_group_check=True)
            return last
        p.step("pe", stm, reads=[f"hs{c}", "ones"] + sqr, writes=statres, deps=entry,
               tag=f"p1.st{c}")

    def evrows(e):
        last = None
        for j in range(NT):
            e.copy(out=sb["row_a"][:, j * 512:(j + 1) * 512], in_=sx[j])
            last = e.copy(out=sb["row_b"][:, j * 512:(j + 1) * 512], in_=sxx[j])
        return last
    p.step("act", evrows, reads=statres, writes=["rows"], tag="p1.evr")

    def ro_a(e):
        e.tensor_scalar(out=sb["row_a"][:], in0=sb["row_a"][:], scalar1=1.0 / D,
                        scalar2=None, op0=ALU.mult)
        return e.tensor_scalar(out=sb["row_b"][:], in0=sb["row_b"][:], scalar1=1.0 / D,
                               scalar2=None, op0=ALU.mult)
    p.step("dve", ro_a, reads=["rows"], writes=["rows"], tag="p1.roa")
    p.step("dve", lambda e: e.tensor_mul(sb["row_c"][:], sb["row_a"][:], sb["row_a"][:]),
           reads=["rows"], writes=["rows"], tag="p1.rob")
    p.step("dve", lambda e: e.tensor_scalar(
        out=sb["row_b"][:], in0=sb["row_b"][:], scalar1=1e-5,
        scalar2=None, op0=ALU.add) if False else e.tensor_sub(
        sb["row_b"][:], sb["row_b"][:], sb["row_c"][:]),
           reads=["rows"], writes=["rows"], tag="p1.roc")
    p.step("dve", lambda e: e.tensor_scalar(out=sb["row_b"][:], in0=sb["row_b"][:],
                                            scalar1=1e-5, scalar2=None, op0=ALU.add),
           reads=["rows"], writes=["rows"], tag="p1.rod")
    p.step("act", lambda e: e.activation(out=sb["row_b"][:], in_=sb["row_b"][:],
                                         func=AF.Ln),
           reads=["rows"], writes=["rows"], tag="p1.rs1")
    p.step("act", lambda e: e.activation(out=sb["row_b"][:], in_=sb["row_b"][:],
                                         func=AF.Exp, scale=-0.5),
           reads=["rows"], writes=["rows"], tag="p1.rs2")

    p.step("dve", lambda e: e.tensor_mul(sb["row_c"][:], sb["row_a"][:], sb["row_b"][:]),
           reads=["rows"], writes=["rows"], tag="p1.mr")

    for j in range(NT):
        def bc(e, j=j):
            e.matmul(ps["C"][:, :], sb["onesrow_f"][:, :],
                     sb["row_c"][:, j * 512:(j + 1) * 512], start=True, stop=True)
            return e.matmul(ps["D"][:, :], sb["onesrow_f"][:, :],
                            sb["row_b"][:, j * 512:(j + 1) * 512], start=True, stop=True)
        p.step("pe", bc, reads=["rows", "ones"], writes=["psC", "psD"], tag=f"p1.bc{j}")

        def evbc(e):
            e.tensor_copy(out=sb["mrow_t"][:], in_=ps["C"][:, :])
            return e.tensor_copy(out=sb["rrow_t"][:], in_=ps["D"][:, :])
        p.step("dve", evbc, reads=["psC", "psD"],
               writes=[f"bcast{j}", "bcast"] + ([f"bcast{j-1}"] if j > 0 else []),
               tag=f"p1.eb{j}")

        for c in range(NCH):
            eng = "dve" if c % 2 == 0 else "pool"
            stgv, stgr2 = fview(c % 2 * 512, c % 2 * 512 + 512)
            sl = slice(c * S + j * 512, c * S + (j + 1) * 512)

            def norm1(e, sl=sl, stgv=stgv):
                return e.tensor_mul(stgv[:, :], hs[:, sl], sb["rrow_t"][:])
            p.step(eng, norm1, reads=[f"bcast{j}", f"hs{c}"], writes=stgr2,
                   tag=f"p1.nm{c}.{j}")

            def norm2(e, sl=sl, stgv=stgv):
                return e.tensor_sub(hs[:, sl], stgv[:, :], sb["mrow_t"][:])
            p.step(eng, norm2, reads=[f"bcast{j}"] + stgr2,
                   writes=[f"hs{c}", f"hs{c}t{j}"], tag=f"p1.nn{c}.{j}")


def phase2_qkv(env):
    io, sb, ps, p, bview = env["io"], env["sb"], env["ps"], env["p"], env["bview"]
    hs = sb["big0"]
    entry = p.barrier_deps()
    p2sids = {"qk": {}, "w16": {}, "v": {}}

    wblk = {}
    wblk["q"] = bview(10240, 18432)
    wblk["k"] = bview(18432, 26624)
    stage = {0: (bview(8192, 8704)[0], ["stg0"]), 1: (bview(8704, 9216)[0], ["stg1"])}
    sqbv = {0: (bview(9216, 9728)[0], ["sqb0"]), 1: (bview(9728, 10240)[0], ["sqb1"])}
    SSP = {0: ps["A"], 1: ps["B"]}
    QSP = {0: (ps["E"], "psE"), 1: (ps["C"], "psC")}
    SPP = {0: (ps["F"], "psF"), 1: (ps["D"], "psD")}
    w1a, w1b = sb["w16a"][0:8, 0:S], sb["w16a"][0:8, S:2 * S]
    w2a, w2b = sb["w16b"][0:8, 0:S], sb["w16b"][0:8, S:2 * S]
    feat = {0: sb["featbuf"][0:8, 0:512], 1: sb["featbuf"][0:8, 512:1024]}
    feat2 = {0: sb["featbuf"][0:8, 1024:1536], 1: sb["featbuf"][0:8, 1536:2048]}

    groups = []
    for b in range(4):
        for side in ("q", "k"):
            for h4 in range(4):
                for ti in range(NT):
                    groups.append((side, b, h4, ti))
    NG = len(groups)

    def emit_chain(gi):
        side, b, h4, ti = groups[gi]
        h = 4 * b + h4
        buf, bufr = wblk[side]
        if h4 == 0 and ti == 0:
            def emit_load(side_, b_):
                wdr = io["wqT"] if side_ == "q" else io["wkT"]
                buf2, bufr2 = wblk[side_]
                qeng = "sync" if side_ == "q" else "act"
                p.step(qeng, lambda e, wdr=wdr, buf2=buf2, b_=b_:
                       e.dma_start(out=buf2[:, :],
                                   in_=wdr[b_ * 128:(b_ + 1) * 128, :]),
                       writes=bufr2, tag=f"p2.w{side_}{b_}", dma=True)
            if side == "q" and b == 0:
                emit_load("q", 0)
            if side == "q":
                emit_load("k", b)
            elif b + 1 < 4:
                emit_load("q", b + 1)
        par = gi % 2
        pq = SSP[par]
        resp = "psA" if par == 0 else "psB"
        stg, stgr = stage[par]
        sqb, sqbr = sqbv[par]

        def mms(e, buf=buf, pq=pq, h4=h4, ti=ti):
            last = None
            for cc in range(NCH):
                c0 = cc * 512 + h4 * 128
                last = e.matmul(pq[:, 0:512], buf[:, c0:c0 + 128],
                                hs[:, cc * S + ti * 512: cc * S + (ti + 1) * 512],
                                start=(cc == 0), stop=(cc == NCH - 1))
            return last
        p.step("pe", mms, reads=bufr + [f"hs{c}t{ti}" for c in range(NCH)],
               writes=[resp], tag=f"p2.{side}{h}.mm{ti}")

    def emit_post(gi):
        side, b, h4, ti = groups[gi]
        h = 4 * b + h4
        par = gi % 2
        pq = SSP[par]
        resp = "psA" if par == 0 else "psB"
        stg, stgr = stage[par]
        sqb, sqbr = sqbv[par]
        scale = INVSQ if side == "q" else 1.0
        p.step("act", lambda e, pq=pq, stg=stg, scale=scale:
               e.mul(out=stg[:, :], in_=pq[:, 0:512], mul=scale),
               reads=[resp], writes=stgr, tag=f"p2.{side}{h}.ev{ti}")

        dst = io["q_dram"] if side == "q" else io["k_dram"]
        p2sids["qk"][(side, h, ti)] = p.step(
            "sync", lambda e, dst=dst, h=h, ti=ti, stg=stg:
            e.dma_start(out=dst[h * 128:(h + 1) * 128, ti * 512:(ti + 1) * 512],
                        in_=stg[:, :]),
            reads=stgr, tag=f"p2.{side}{h}.st{ti}")
        p.step("dve", lambda e, stg=stg, sqb=sqb:
               e.tensor_mul(sqb[:, 0:512], stg[:, :], stg[:, :]),
               reads=stgr, writes=sqbr, tag=f"p2.{side}{h}.sq{ti}")

    def emit_qs(gi):
        side, b, h4, ti = groups[gi]
        h = 4 * b + h4
        par = gi % 2
        sqb, sqbr = sqbv[par]
        qp, qpr = QSP[par]
        p.step("pe", lambda e, sqb=sqb, qp=qp:
               e.matmul(qp[0:1, 0:512], sb["onescol_b"][:, :], sqb[:, 0:512],
                        start=True, stop=True),
               reads=sqbr + ["ones"], writes=[qpr], tag=f"p2.{side}{h}.qs{ti}")
        p.step("dve", lambda e, qp=qp, par=par:
               e.tensor_copy(out=sb["qsq_pair"][0:1, par * 512:par * 512 + 512],
                             in_=qp[0:1, 0:512]),
               reads=[qpr], writes=[f"qsq{par}"], tag=f"p2.{side}{h}.qr{ti}")

    def emit_spmm(gi):
        side, b, h4, ti = groups[gi]
        h = 4 * b + h4
        par = gi % 2
        stg, stgr = stage[par]
        sp, spr = SPP[par]
        r1 = sb["r1q"] if side == "q" else sb["r1k"]
        rc = sb["rcq"] if side == "q" else sb["rck"]

        def spmm(e, r1=r1, rc=rc, h=h, stg=stg, sp=sp, par=par):
            e.matmul(sp[0:64, :], r1[:, h * 64:(h + 1) * 64],
                     stg[:, :], start=True, stop=False)
            return e.matmul(sp[0:64, :], rc[:, h * 64:(h + 1) * 64],
                            sb["qsq_pair"][:, par * 512:par * 512 + 512],
                            start=False, stop=True)
        p.step("pe", spmm, reads=["r1", "rc", f"qsq{par}"] + stgr,
               writes=[spr], tag=f"p2.{side}{h}.sp{ti}")

        ft = feat[par]
        if side == "q":
            def spev(e, h=h, ti=ti, sp=sp, ft=ft):
                e.activation(out=w1a[:, ti * 512:(ti + 1) * 512],
                             in_=sp[0:8, :], func=AF.Exp,
                             bias=sb["lamT"][:, h:h + 1])
                return e.activation(out=ft[:, :], in_=sp[32:40, :], func=AF.Tanh)
            p.step("act", spev, reads=[spr, "lamT"], writes=["w16a", f"feat{par}"],
                   tag=f"p2.q{h}.se{ti}")
            p.step("dve", lambda e, ti=ti, ft=ft:
                   e.scalar_tensor_tensor(out=w1b[:, ti * 512:(ti + 1) * 512],
                                          in0=w1a[:, ti * 512:(ti + 1) * 512],
                                          scalar=sb["dsv8"][:, 0:1],
                                          in1=ft[:, :],
                                          op0=ALU.mult, op1=ALU.mult),
                   reads=["w16a", f"feat{par}", "dsv8"], writes=["w16a"],
                   tag=f"p2.q{h}.sd{ti}")
        else:
            f2 = feat2[par]

            def spev(e, sp=sp, ft=ft, f2=f2):
                e.activation(out=f2[:, :], in_=sp[0:8, :], func=AF.Exp)
                return e.activation(out=ft[:, :], in_=sp[32:40, :], func=AF.Tanh)
            p.step("act", spev, reads=[spr], writes=[f"feat{par}", f"feat2{par}"],
                   tag=f"p2.k{h}.se{ti}")

            def spd(e, ti=ti, ft=ft, f2=f2):
                e.tensor_scalar(out=w2a[:, ti * 512:(ti + 1) * 512],
                                in0=f2[:, :], scalar1=sb["psv8"][:, 0:1],
                                scalar2=None, op0=ALU.mult)
                return e.scalar_tensor_tensor(out=w2b[:, ti * 512:(ti + 1) * 512],
                                              in0=f2[:, :],
                                              scalar=sb["psv8"][:, 0:1],
                                              in1=ft[:, :],
                                              op0=ALU.mult, op1=ALU.mult)
            p.step("dve", spd, reads=[f"feat{par}", f"feat2{par}", "psv8"],
                   writes=["w16b"], tag=f"p2.k{h}.sd{ti}")

        if ti == NT - 1:
            wnm = "w1_dram" if side == "q" else "w2_dram"
            pairs = ((w1a, 0), (w1b, 8)) if side == "q" else ((w2a, 0), (w2b, 8))
            for buf_, r0 in pairs:
                sid = p.step("sync", lambda e, wnm=wnm, buf_=buf_, r0=r0, h=h:
                             e.dma_start(out=io[wnm][h * 16 + r0:h * 16 + r0 + 8, :],
                                         in_=buf_[:, :]),
                             reads=["w16a" if side == "q" else "w16b"],
                             tag=f"p2.st{wnm}{h}.{r0}")
                p2sids["w16"].setdefault((side, h), []).append(sid)

    for gi in range(NG + 2):
        if gi < NG:
            emit_chain(gi)
        if 0 <= gi - 1 < NG:
            emit_qs(gi - 1)
        if 0 <= gi - 2 < NG:
            emit_spmm(gi - 2)
        if gi < NG:
            emit_post(gi)

    wv = {0: wblk["q"], 1: wblk["k"]}
    for fj in range(NT):
        par = fj % 2
        wvb, wvr = wv[par]
        p.step("gp", lambda e, fj=fj, wvb=wvb:
               e.dma_start(out=wvb[:, :],
                           in_=io["wvT"][fj * 128:(fj + 1) * 128, :]),
               writes=wvr, tag=f"p2.wv{fj}")
        for tc in range(NCH):
            par2 = tc % 2
            pv = ps["A"] if par2 == 0 else ps["B"]
            resp = "psA" if par2 == 0 else "psB"
            stg, stgr = stage[par2]

            def vmm(e, tc=tc, wvb=wvb, pv=pv):
                last = None
                for cc in range(NCH):
                    last = e.matmul(pv[:, 0:512],
                                    hs[:, cc * S + tc * 128: cc * S + (tc + 1) * 128],
                                    wvb[:, cc * 512:(cc + 1) * 512],
                                    start=(cc == 0), stop=(cc == NCH - 1))
                return last
            p.step("pe", vmm, reads=wvr + [f"hs{c}t{tc // 4}" for c in range(NCH)],
                   writes=[resp], tag=f"p2.v{fj}.mm{tc}")
            p.step("act", lambda e, pv=pv, stg=stg: e.copy(out=stg[:, :], in_=pv[:, 0:512]),
                   reads=[resp], writes=stgr, tag=f"p2.v{fj}.ev{tc}")
            p2sids["v"].setdefault(fj, []).append(p.step(
                "sync", lambda e, fj=fj, tc=tc, stg=stg:
                e.dma_start(out=io["v_dram"][tc * 128:(tc + 1) * 128,
                                             fj * 512:(fj + 1) * 512], in_=stg[:, :]),
                reads=stgr, tag=f"p2.v{fj}.st{tc}"))
    return p2sids


def phase4_attn(env, p2sids=None):
    io, sb, ps, p, bview, fview = (env["io"], env["sb"], env["ps"], env["p"],
                                   env["bview"], env["fview"])
    entry = p.barrier_deps()
    big0 = sb["big0"]

    qh = {0: bview(0, 2048), 1: bview(2048, 4096)}
    kh = {0: bview(4096, 6144), 1: bview(6144, 8192)}
    vh = {0: bview(8192, 10240), 1: bview(10240, 12288)}
    pout = {i: bview(12288 + 1024 * i, 13312 + 1024 * i) for i in range(4)}
    rs_sb, rsr = fview(0, 1024)
    t_sb, t_r = fview(1024, 2048)
    w1h = {0: (sb["w16a"][:, 0:S], ["w16a"]), 1: (sb["w16a"][:, S:2 * S], ["w16a2"])}
    w2h = {0: (sb["w16b"][:, 0:S], ["w16b"]), 1: (sb["w16b"][:, S:2 * S], ["w16b2"])}

    SSP = {0: ps["A"], 1: ps["B"]}
    O, L = ps["CD"], ps["EF"]
    ores = ["psC", "psD"]
    lres = ["psE", "psF"]
    deferred = []

    def flush_deferred():
        while deferred:
            deferred.pop(0)()

    for h in range(H):
        hb = h % 2
        qhb, qhr = qh[hb]
        khb, khr = kh[hb]
        vhb, vhr = vh[hb]
        w1b_, w1r = w1h[hb]
        w2b_, w2r = w2h[hb]
        early = h < 2 and p2sids is not None
        ldeng = "act" if early else "sync"
        dq = [p2sids["qk"][("q", h, ti_)] for ti_ in range(NT)] if early else entry
        dk = [p2sids["qk"][("k", h, ti_)] for ti_ in range(NT)] if early else entry
        dv = entry
        d1 = p2sids["w16"][("q", h)] if early else entry
        d2 = p2sids["w16"][("k", h)] if early else entry
        veng = "sync"
        p.step(ldeng, lambda e, h=h, qhb=qhb:
               e.dma_start(out=qhb[:, :], in_=io["q_dram"][h * 128:(h + 1) * 128, :]),
               writes=qhr, deps=dq, tag=f"p4.ldq{h}", dma=True)
        p.step(ldeng, lambda e, h=h, khb=khb:
               e.dma_start(out=khb[:, :], in_=io["k_dram"][h * 128:(h + 1) * 128, :]),
               writes=khr, deps=dk, tag=f"p4.ldk{h}", dma=True)
        p.step(veng, lambda e, h=h, vhb=vhb:
               e.dma_start(out=vhb.rearrange("p (tc f) -> p tc f", f=128),
                           in_=io["v_dram"][:, h * 128:(h + 1) * 128]
                           .rearrange("(tc p) f -> p tc f", p=128)),
               writes=vhr, deps=dv, tag=f"p4.ldv{h}", dma=True)
        p.step(ldeng, lambda e, h=h, w1b_=w1b_:
               e.dma_start(out=w1b_[:, :], in_=io["w1_dram"][h * 16:(h + 1) * 16, :]),
               writes=w1r, deps=d1, tag=f"p4.ldw1{h}", dma=True)
        p.step(ldeng, lambda e, h=h, w2b_=w2b_:
               e.dma_start(out=w2b_[:, :], in_=io["w2_dram"][h * 16:(h + 1) * 16, :]),
               writes=w2r, deps=d2, tag=f"p4.ldw2{h}", dma=True)

        for ti in range(NT):
            nj = 4 * ti + 4

            def mk_avmm(j, vhb=vhb):
                pob, por = pout[j % 4]

                def avmm(e, vhb=vhb, j=j, pob=pob,
                         first=(j == 0), last_=(j == nj - 1)):
                    vj = vhb[:, j * 128:(j + 1) * 128]
                    e.matmul(O[:, 0:512], vj, pob[:, 0:512],
                             start=first, stop=last_, skip_group_check=True)
                    e.matmul(O[:, 512:1024], vj, pob[:, 512:1024],
                             start=first, stop=last_, skip_group_check=True)
                    e.matmul(L[:, 0:512], sb["ones_b"][:, :], pob[:, 0:512],
                             start=first, stop=last_, skip_group_check=True)
                    return e.matmul(L[:, 512:1024], sb["ones_b"][:, :],
                                    pob[:, 512:1024],
                                    start=first, stop=last_, skip_group_check=True)
                p.step("pe", avmm, reads=por + vhr + ["ones"], writes=ores + lres,
                       tag=f"p4.{h}.{ti}.a{j}")

            for j in range(nj):
                par = j % 2
                sspr = "psA" if par == 0 else "psB"
                pob, por = pout[j % 4]
                diag = j >= 4 * ti

                def smm(e, qhb=qhb, khb=khb, w1b_=w1b_, w2b_=w2b_, ti=ti, j=j,
                        par=par, diag=diag):
                    S_ = SSP[par]
                    e.matmul(S_[:, 0:512], khb[:, j * 128:(j + 1) * 128],
                             qhb[:, ti * 512:(ti + 1) * 512], start=True,
                             stop=not diag, skip_group_check=True)
                    e.matmul(S_[:, 512:1024], khb[:, j * 128:(j + 1) * 128],
                             qhb[:, ti * 512:(ti + 1) * 512], start=True,
                             stop=False, skip_group_check=True)
                    last = e.matmul(S_[:, 512:1024],
                                    w2b_[:, j * 128:(j + 1) * 128],
                                    w1b_[:, ti * 512:(ti + 1) * 512],
                                    start=False, stop=not diag,
                                    skip_group_check=True)
                    if diag:
                        d = j - 4 * ti
                        e.matmul(S_[:, 0:512], sb["ident_b"][:, :],
                                 sb["masks"][:, d * 512:(d + 1) * 512],
                                 start=False, stop=True, skip_group_check=True)
                        last = e.matmul(S_[:, 512:1024], sb["ident_b"][:, :],
                                        sb["masks"][:, d * 512:(d + 1) * 512],
                                        start=False, stop=True,
                                        skip_group_check=True)
                    return last
                p.step("pe", smm, reads=qhr + khr + w1r + w2r + ["masks", "ident_b"],
                       writes=[sspr], tag=f"p4.{h}.{ti}.s{j}")

                if j >= 2:
                    mk_avmm(j - 2)

                p.step("act", lambda e, par=par, h=h, pob=pob:
                       e.activation(out=pob[:, :], in_=SSP[par][:, :], func=AF.Exp,
                                    bias=sb["negub"][:, h:h + 1]),
                       reads=[sspr, "negub"], writes=por, tag=f"p4.{h}.{ti}.e{j}")

                if j == 2:
                    flush_deferred()

            for jt in range(max(0, nj - 2), nj):
                mk_avmm(jt)

            p.step("dve", lambda e: e.tensor_copy(out=t_sb[:, :], in_=O[:, :]),
                   reads=ores, writes=t_r, tag=f"p4.{h}.{ti}.e1")
            p.step("dve", lambda e: e.tensor_copy(out=rs_sb[:, :], in_=L[:, :]),
                   reads=lres, writes=rsr, tag=f"p4.{h}.{ti}.e2")

            def mk_tail(h=h, ti=ti):
                p.step("act", lambda e: e.activation(out=rs_sb[:, :], in_=rs_sb[:, :],
                                                     func=AF.Ln),
                       reads=rsr, writes=rsr, tag=f"p4.{h}.{ti}.e3")

                def ep_r(e):
                    e.activation(out=rs_sb[:, 0:512], in_=rs_sb[:, 0:512], func=AF.Exp,
                                 scale=-1.0, bias=sb["lnb"][:, 0:1])
                    return e.activation(out=rs_sb[:, 512:1024],
                                        in_=rs_sb[:, 512:1024],
                                        func=AF.Exp, scale=-1.0,
                                        bias=sb["lnb"][:, 1:2])
                p.step("act", ep_r, reads=rsr + ["lnb"], writes=rsr,
                       tag=f"p4.{h}.{ti}.e3b")

                def ep_m(e):
                    e.tensor_mul(t_sb[:, 0:512], t_sb[:, 0:512], rs_sb[:, 0:512])
                    return e.tensor_mul(t_sb[:, 512:1024], t_sb[:, 512:1024],
                                        rs_sb[:, 512:1024])
                p.step("dve", ep_m, reads=rsr + t_r, writes=t_r,
                       tag=f"p4.{h}.{ti}.e4")
                p.step("pool", lambda e:
                       e.tensor_add(big0[:, h * S + ti * 512: h * S + (ti + 1) * 512],
                                    t_sb[:, 0:512], t_sb[:, 512:1024]),
                       reads=t_r, writes=[f"o{h}"], tag=f"p4.{h}.{ti}.o")
            deferred.append(mk_tail)
    flush_deferred()


def phase5_proj(env, w_dram, x_dram, dst, mk_b16=False):
    io, sb, ps, p, bview, fview = (env["io"], env["sb"], env["ps"], env["p"],
                                   env["bview"], env["fview"])
    src = sb["big0"]
    entry = p.barrier_deps()
    b16_sids = {}

    y = {0: bview(0, 8192)[0], 1: bview(8192, 16384)[0]}
    y_r = {(jj, ec): [f"p5y{jj}c{ec}"] for jj in range(2) for ec in range(NCH)}
    wbuf = {0: (bview(16384, 20480)[0], ["p5w0"]), 1: (bview(20480, 24576)[0], ["p5w1"])}
    sqv = {(jj, pr): (bview(24576 + 512 * (2 * jj + pr),
                            25088 + 512 * (2 * jj + pr))[0], [f"p5sq{jj}{pr}"])
           for jj in range(2) for pr in range(2)}
    xv = {0: fview(0, 512), 1: fview(512, 1024)}
    x1v = {0: fview(1024, 1536), 1: fview(1536, 2048)}
    psY = {0: ps["A"], 1: ps["B"]}
    sxs = {0: ps["E"][0:1, :], 1: ps["E"][64:65, :]}
    sxx = {0: ps["F"][0:1, :], 1: ps["F"][64:65, :]}

    def emit_block(jp, eb):
        wb, wbr = wbuf[eb % 2]
        blks = ([f"sbB{i}" for i in range(16, 20)] if eb % 2 == 0 else
                [f"sbB{i}" for i in range(20, 24)])
        p.step("sync" if eb < 2 else ("gp" if eb % 2 == 0 else "sync"),
               lambda e, eb=eb, wb=wb:
               e.dma_start(out=wb[:, :],
                           in_=w_dram[eb * 128:(eb + 1) * 128, :]),
               writes=wbr + blks + ["stg0", "stg1"],
               tag=f"p5.{jp}.w{eb}", dma=True)
        for sub in range(2):
            ec = 2 * eb + sub
            for jj in range(2):
                j = 2 * jp + jj
                pv = psY[jj]
                resp = "psA" if jj == 0 else "psB"

                def ymm(e, j=j, wb=wb, pv=pv, sub=sub):
                    last = None
                    for cc in range(NCH):
                        c0 = cc * 256 + sub * 128
                        last = e.matmul(pv[:, 0:512], wb[:, c0:c0 + 128],
                                        src[:, cc * S + j * 512: cc * S + (j + 1) * 512],
                                        start=(cc == 0), stop=(cc == NCH - 1))
                    return last
                p.step("pe", ymm, reads=wbr, writes=[resp], deps=entry,
                       tag=f"p5.{j}.mm{ec}")
            if ec >= 1:
                emit_stm(jp, ec - 1)
            for jj in range(2):
                j = 2 * jp + jj
                pv = psY[jj]
                resp = "psA" if jj == 0 else "psB"
                p.step("act", lambda e, pv=pv, ec=ec, jj=jj:
                       e.copy(out=y[jj][:, ec * 512:(ec + 1) * 512], in_=pv[:, 0:512]),
                       reads=[resp], writes=y_r[(jj, ec)], tag=f"p5.{j}.ev{ec}")
                sq, sqr = sqv[(jj, ec % 2)]
                p.step("dve", lambda e, ec=ec, jj=jj, sq=sq:
                       e.tensor_mul(sq[:, 0:512], y[jj][:, ec * 512:(ec + 1) * 512],
                                    y[jj][:, ec * 512:(ec + 1) * 512]),
                       reads=y_r[(jj, ec)], writes=sqr, tag=f"p5.{j}.sq{ec}")

    def emit_stm(jp, ec):
        for jj in range(2):
            j = 2 * jp + jj
            sq, sqr = sqv[(jj, ec % 2)]

            def stm(e, ec=ec, jj=jj, sq=sq):
                e.matmul(sxs[jj], sb["onescol_b"][:, :],
                         y[jj][:, ec * 512:(ec + 1) * 512],
                         start=(ec == 0), stop=(ec == NCH - 1), skip_group_check=True)
                return e.matmul(sxx[jj], sb["onescol_b"][:, :], sq[:, 0:512],
                                start=(ec == 0), stop=(ec == NCH - 1),
                                skip_group_check=True)
            p.step("pe", stm, reads=y_r[(jj, ec)] + sqr + ["ones"],
                   writes=["psE", "psF"], tag=f"p5.{j}.st{ec}")

    def emit_rows(jp):
        emit_stm(jp, NCH - 1)

        def evr(e):
            e.copy(out=sb["row_a"][:, 0:512], in_=sxs[0])
            e.copy(out=sb["row_a"][:, 512:1024], in_=sxs[1])
            e.copy(out=sb["row_b"][:, 0:512], in_=sxx[0])
            return e.copy(out=sb["row_b"][:, 512:1024], in_=sxx[1])
        p.step("act", evr, reads=["psE", "psF"], writes=["rows"], tag=f"p5.{jp}.er")

        def ro_a(e):
            e.tensor_scalar(out=sb["row_a"][:, 0:1024], in0=sb["row_a"][:, 0:1024],
                            scalar1=1.0 / D, scalar2=None, op0=ALU.mult)
            return e.tensor_scalar(out=sb["row_b"][:, 0:1024], in0=sb["row_b"][:, 0:1024],
                                   scalar1=1.0 / D, scalar2=None, op0=ALU.mult)
        p.step("dve", ro_a, reads=["rows"], writes=["rows"], tag=f"p5.{jp}.roa")
        p.step("dve", lambda e: e.tensor_mul(sb["row_c"][:, 0:1024],
                                             sb["row_a"][:, 0:1024],
                                             sb["row_a"][:, 0:1024]),
               reads=["rows"], writes=["rows"], tag=f"p5.{jp}.rob")
        p.step("dve", lambda e: e.tensor_sub(sb["row_b"][:, 0:1024],
                                             sb["row_b"][:, 0:1024],
                                             sb["row_c"][:, 0:1024]),
               reads=["rows"], writes=["rows"], tag=f"p5.{jp}.roc")
        p.step("dve", lambda e: e.tensor_scalar(out=sb["row_b"][:, 0:1024],
                                                in0=sb["row_b"][:, 0:1024],
                                                scalar1=1e-5, scalar2=None, op0=ALU.add),
               reads=["rows"], writes=["rows"], tag=f"p5.{jp}.rod")
        p.step("act", lambda e: e.activation(out=sb["row_b"][:, 0:1024],
                                             in_=sb["row_b"][:, 0:1024], func=AF.Ln),
               reads=["rows"], writes=["rows"], tag=f"p5.{jp}.rs1")
        p.step("act", lambda e: e.activation(out=sb["row_b"][:, 0:1024],
                                             in_=sb["row_b"][:, 0:1024], func=AF.Exp,
                                             scale=-0.5),
               reads=["rows"], writes=["rows"], tag=f"p5.{jp}.rs2")

    def emit_norm(jp, jj):
        j = 2 * jp + jj
        half = jj * 512

        def bc(e, half=half):
            e.matmul(ps["C"][:, :], sb["onesrow_f"][:, :],
                     sb["row_a"][:, half:half + 512], start=True, stop=True)
            return e.matmul(ps["D"][:, :], sb["onesrow_f"][:, :],
                            sb["row_b"][:, half:half + 512], start=True, stop=True)
        p.step("pe", bc, reads=["rows", "ones"], writes=["psC", "psD"],
               tag=f"p5.{j}.bc")

        def eb(e):
            e.tensor_copy(out=sb["mrow_t"][:], in_=ps["C"][:, :])
            return e.tensor_copy(out=sb["rrow_t"][:], in_=ps["D"][:, :])
        p.step("dve", eb, reads=["psC", "psD"], writes=["bcast"], tag=f"p5.{j}.eb")

        for ec in range(NCH):
            par2 = ec % 2
            x_sb, x_r = xv[par2]
            x1_sb, x1r = x1v[par2]
            p.step("sync", lambda e, ec=ec, j=j, x_sb=x_sb:
                   e.dma_start(out=x_sb[:, :],
                               in_=x_dram[ec * 128:(ec + 1) * 128,
                                          j * 512:(j + 1) * 512]),
                   writes=x_r, deps=entry, tag=f"p5.{j}.lx{ec}")

            neng = "dve"
            p.step(neng, lambda e, ec=ec, x1_sb=x1_sb, jj=jj: e.tensor_sub(
                x1_sb[:, :], y[jj][:, ec * 512:(ec + 1) * 512], sb["mrow_t"][:]),
                   reads=["bcast"] + y_r[(jj, ec)], writes=x1r, tag=f"p5.{j}.nm{ec}")
            p.step(neng, lambda e, x1_sb=x1_sb: e.tensor_mul(
                x1_sb[:, :], x1_sb[:, :], sb["rrow_t"][:]),
                   reads=["bcast"] + x1r, writes=x1r, tag=f"p5.{j}.nn{ec}")
            p.step(neng, lambda e, x1_sb=x1_sb, x_sb=x_sb: e.tensor_add(
                x1_sb[:, :], x1_sb[:, :], x_sb[:, :]),
                   reads=x1r + x_r, writes=x1r, tag=f"p5.{j}.na{ec}")

            p.step("act", lambda e, ec=ec, j=j, x1_sb=x1_sb:
                   e.dma_start(out=dst[ec * 128:(ec + 1) * 128,
                                       j * 512:(j + 1) * 512],
                               in_=x1_sb[:, :]),
                   reads=x1r, tag=f"p5.{j}.sx{ec}", dma=True)
            if mk_b16:
                p.step("dve", lambda e, ec=ec, jj=jj, x1_sb=x1_sb:
                       e.tensor_copy(out=y[jj][:, ec * 512:(ec + 1) * 512],
                                     in_=x1_sb[:, :]),
                       reads=x1r, writes=y_r[(jj, ec)], tag=f"p5.{j}.cb{ec}")
                sid = p.step("gp", lambda e, ec=ec, j=j, jj=jj:
                             e.dma_start(out=io["x1b_dram"][ec * 128:(ec + 1) * 128,
                                                            j * 512:(j + 1) * 512],
                                         in_=y[jj][:, ec * 512:(ec + 1) * 512]),
                             reads=y_r[(jj, ec)], tag=f"p5.{j}.sb{ec}")
                b16_sids[(j, ec)] = sid

    for eb_ in range(8):
        emit_block(0, eb_)
    emit_rows(0)
    emit_norm(0, 0)
    emit_norm(0, 1)
    for eb_ in range(8):
        emit_block(1, eb_)
    emit_rows(1)
    emit_norm(1, 0)
    emit_norm(1, 1)
    return b16_sids


def phase7_ffnproj(env):
    io, sb, ps, p, bview = env["io"], env["sb"], env["ps"], env["p"], env["bview"]
    hs = sb["big0"]
    entry = p.barrier_deps()

    wblk = {"q": bview(0, 8192), "k": bview(8192, 16384)}
    stage = {0: (bview(16384, 16896)[0], ["stg0"]), 1: (bview(16896, 17408)[0], ["stg1"])}

    for b in range(4):
        for side in ("q", "k"):
            buf, bufr = wblk[side]

            def emit_load7(side_, b_):
                wdr = io["wa_qa"] if side_ == "q" else io["wa_ka"]
                buf2, bufr2 = wblk[side_]
                qeng = "gp" if side_ == "q" else "act"
                p.step(qeng, lambda e, wdr=wdr, buf2=buf2, b_=b_:
                       e.dma_start(out=buf2[:, :],
                                   in_=wdr[b_ * 128:(b_ + 1) * 128, :]),
                       writes=bufr2, tag=f"p7.w{side_}{b_}", dma=True)
            if side == "q" and b == 0:
                emit_load7("q", 0)
            if side == "q":
                emit_load7("k", b)
            elif b + 1 < 4:
                emit_load7("q", b + 1)
            for h4 in range(4):
                h = 4 * b + h4
                for ti in range(NT):
                    par = ti % 2
                    pq = ps["A"] if par == 0 else ps["B"]
                    resp = "psA" if par == 0 else "psB"
                    stg, stgr = stage[par]

                    def mms(e, buf=buf, pq=pq, ti=ti, h4=h4):
                        last = None
                        for cc in range(NCH):
                            c0 = cc * 512 + h4 * 128
                            last = e.matmul(pq[:, 0:512], buf[:, c0:c0 + 128],
                                            hs[:, cc * S + ti * 512:
                                               cc * S + (ti + 1) * 512],
                                            start=(cc == 0), stop=(cc == NCH - 1))
                        return last
                    p.step("pe", mms,
                           reads=bufr + [f"hs{c}t{ti}" for c in range(NCH)],
                           writes=[resp], tag=f"p7.{side}{h}.mm{ti}")

                    sc = sb["qsc"] if side == "q" else sb["ksc"]
                    p.step("act", lambda e, pq=pq, stg=stg, sc=sc, h=h:
                           e.activation(out=stg[:, :], in_=pq[:, 0:512], func=AF.Copy,
                                        scale=sc[:, h:h + 1]),
                           reads=[resp, "qsc"], writes=stgr, tag=f"p7.{side}{h}.ev{ti}")

                    dstd = io["qf_dram"] if side == "q" else io["kf_dram"]
                    p.step("sync", lambda e, dstd=dstd, h=h, ti=ti, stg=stg:
                           e.dma_start(out=dstd[h * 128:h * 128 + 120,
                                                ti * 512:(ti + 1) * 512],
                                       in_=stg[0:120, :]),
                           reads=stgr, tag=f"p7.{side}{h}.st{ti}")

    wl, wlr = bview(17408, 21504)
    qlowT, qlr = bview(21504, 23552)
    klowT, klr = bview(23552, 25600)
    tails, tlr = bview(0, 4096)

    p.step("gp", lambda e:
           e.dma_start(out=wl[:, :], in_=io["wa_low"][:, :]),
           writes=wlr, deps=entry, tag="p7.wl")
    for tc in range(NCH):
        def lmm(e, tc=tc):
            last = None
            for cc in range(NCH):
                last = e.matmul(ps["E"][:, 0:256],
                                hs[:, cc * S + tc * 128: cc * S + (tc + 1) * 128],
                                wl[:, cc * 256:(cc + 1) * 256],
                                start=(cc == 0), stop=(cc == NCH - 1))
            return last
        p.step("pe", lmm, reads=wlr + [f"hs{c}t{tc // 4}" for c in range(NCH)],
               writes=["psE"], tag=f"p7.lo.mm{tc}")
        p.step("act", lambda e: e.copy(out=sb["mrow_t"][:, 0:256], in_=ps["E"][:, 0:256]),
               reads=["psE"], writes=["lowtm_f"], tag=f"p7.lo.ev{tc}")

        def ltr(e):
            e.transpose(ps["F"][:, 0:128], sb["mrow_t"][:, 0:128], sb["ident_f"][:, :])
            return e.transpose(ps["F"][:, 128:256], sb["mrow_t"][:, 128:256],
                               sb["ident_f"][:, :])
        p.step("pe", ltr, reads=["lowtm_f", "ident_f"], writes=["psF"], tag=f"p7.lo.tr{tc}")

        def lev(e, tc=tc):
            e.tensor_copy(out=qlowT[:, tc * 128:(tc + 1) * 128], in_=ps["F"][:, 0:128])
            return e.tensor_copy(out=klowT[:, tc * 128:(tc + 1) * 128],
                                 in_=ps["F"][:, 128:256])
        p.step("dve", lev, reads=["psF"], writes=qlr + klr, tag=f"p7.lo.le{tc}")

    def tail_a(e):
        e.tensor_scalar(out=tails[:, 0:2048], in0=klowT[:, :],
                        scalar1=sb["tailsc"][:, 1:2], scalar2=None, op0=ALU.mult)
        return e.tensor_scalar(out=tails[:, 2048:4096], in0=qlowT[:, :],
                               scalar1=sb["tailsc"][:, 3:4], scalar2=None, op0=ALU.mult)
    p.step("dve", tail_a, reads=qlr + klr + ["tailsc"], writes=tlr, tag="p7.tail_a")

    def tail_b(e):
        e.scalar_tensor_tensor(out=tails[:, 0:2048], in0=qlowT[:, :],
                               scalar=sb["tailsc"][:, 0:1], in1=tails[:, 0:2048],
                               op0=ALU.mult, op1=ALU.add)
        return e.scalar_tensor_tensor(out=tails[:, 2048:4096], in0=klowT[:, :],
                                      scalar=sb["tailsc"][:, 2:3], in1=tails[:, 2048:4096],
                                      op0=ALU.mult, op1=ALU.add)
    p.step("dve", tail_b, reads=qlr + klr + ["tailsc"] + tlr, writes=tlr, tag="p7.tail_b")

    for h in range(H):
        p.step("sync", lambda e, h=h:
               e.dma_start(out=io["qf_dram"][h * 128 + 120:h * 128 + 128, :],
                           in_=tails[h * 8:(h + 1) * 8, 0:2048]),
               reads=tlr, tag=f"p7.sttq{h}")
        p.step("sync", lambda e, h=h:
               e.dma_start(out=io["kf_dram"][h * 128 + 120:h * 128 + 128, :],
                           in_=tails[h * 8:(h + 1) * 8, 2048:4096]),
               reads=tlr, tag=f"p7.sttk{h}")

    wv = {0: bview(4096, 12288), 1: bview(12288, 20480)}
    stage2 = {0: (bview(25600, 26112)[0], ["v2s0"]), 1: (bview(26112, 26624)[0], ["v2s1"])}
    for fj in range(NT):
        par = fj % 2
        wvb, wvr = wv[par]
        p.step("gp", lambda e, fj=fj, wvb=wvb:
               e.dma_start(out=wvb[:, :],
                           in_=io["wa_v"][fj * 128:(fj + 1) * 128, :]),
               writes=wvr + (["stg0", "stg1"] if par == 1 else []),
               deps=entry, tag=f"p7.wv{fj}")
        for tc in range(NCH):
            par2 = tc % 2
            pv = ps["A"] if par2 == 0 else ps["B"]
            resp = "psA" if par2 == 0 else "psB"
            stg, stgr = stage2[par2]

            def vmm(e, tc=tc, wvb=wvb, pv=pv):
                last = None
                for cc in range(NCH):
                    last = e.matmul(pv[:, 0:512],
                                    hs[:, cc * S + tc * 128: cc * S + (tc + 1) * 128],
                                    wvb[:, cc * 512:(cc + 1) * 512],
                                    start=(cc == 0), stop=(cc == NCH - 1))
                return last
            p.step("pe", vmm, reads=wvr + [f"hs{c}t{tc // 4}" for c in range(NCH)],
                   writes=[resp], tag=f"p7.v{fj}.mm{tc}")
            p.step("act", lambda e, pv=pv, stg=stg: e.copy(out=stg[:, :], in_=pv[:, 0:512]),
                   reads=[resp], writes=stgr, tag=f"p7.v{fj}.ev{tc}")
            p.step("sync", lambda e, fj=fj, tc=tc, stg=stg:
                   e.dma_start(out=io["v2_dram"][tc * 128:(tc + 1) * 128,
                                                 fj * 512:(fj + 1) * 512], in_=stg[:, :]),
                   reads=stgr, tag=f"p7.v{fj}.st{tc}")


def phase8_ffnattn(env):
    io, sb, ps, p, bview, fview = (env["io"], env["sb"], env["ps"], env["p"],
                                   env["bview"], env["fview"])
    entry = p.barrier_deps()
    big0 = sb["big0"]

    qh = {0: bview(0, 2048), 1: bview(2048, 4096)}
    kh = {0: bview(4096, 6144), 1: bview(6144, 8192)}
    vh = {0: bview(8192, 10240), 1: bview(10240, 12288)}
    pout = {i: bview(12288 + 1024 * i, 13312 + 1024 * i) for i in range(4)}
    rs_sb, rsr = fview(0, 512)

    SSb = {0: ps["A"], 1: ps["B"]}
    O, L = ps["C"], ps["D"]
    deferred8 = []

    for h in range(H):
        hb = h % 2
        qhb, qhr = qh[hb]
        khb, khr = kh[hb]
        vhb, vhr = vh[hb]
        p.step("sync", lambda e, h=h, qhb=qhb:
               e.dma_start(out=qhb[:, :], in_=io["qf_dram"][h * 128:(h + 1) * 128, :]),
               writes=qhr, deps=entry, tag=f"p8.ldq{h}")
        p.step("sync", lambda e, h=h, khb=khb:
               e.dma_start(out=khb[:, :], in_=io["kf_dram"][h * 128:(h + 1) * 128, :]),
               writes=khr, deps=entry, tag=f"p8.ldk{h}")
        p.step("sync", lambda e, h=h, vhb=vhb:
               e.dma_start(out=vhb.rearrange("p (tc f) -> p tc f", f=128),
                           in_=io["v2_dram"][:, h * 128:(h + 1) * 128]
                           .rearrange("(tc p) f -> p tc f", p=128)),
               writes=vhr, deps=entry, tag=f"p8.ldv{h}")

        for ti in range(NT):
            nj = 4 * ti + 4

            def mk_avmm8(jb, vhb=vhb):
                pob, por = pout[(jb // 2) % 4]

                def avmm(e, vhb=vhb, jb=jb, pob=pob,
                         first=(jb == 0), last_=(jb == nj - 2)):
                    e.matmul(O[:, :], vhb[:, jb * 128:(jb + 1) * 128], pob[:, 0:512],
                             start=first, stop=False, skip_group_check=True)
                    e.matmul(O[:, :], vhb[:, (jb + 1) * 128:(jb + 2) * 128],
                             pob[:, 512:1024], start=False, stop=last_,
                             skip_group_check=True)
                    e.matmul(L[:, :], sb["ones_b"][:, :], pob[:, 0:512],
                             start=first, stop=False, skip_group_check=True)
                    return e.matmul(L[:, :], sb["ones_b"][:, :], pob[:, 512:1024],
                                    start=False, stop=last_, skip_group_check=True)
                p.step("pe", avmm, reads=por + vhr + ["ones"], writes=["psC", "psD"],
                       tag=f"p8.{h}.{ti}.a{jb}")

            for jb in range(0, nj, 2):
                par = (jb // 2) % 2
                sspr = "psA" if par == 0 else "psB"
                pob, por = pout[(jb // 2) % 4]

                def smm(e, qhb=qhb, khb=khb, ti=ti, jb=jb, par=par):
                    last = None
                    for jj in (jb, jb + 1):
                        half = (jj - jb) * 512
                        diag = jj >= 4 * ti
                        last = e.matmul(SSb[par][:, half:half + 512],
                                        khb[:, jj * 128:(jj + 1) * 128],
                                        qhb[:, ti * 512:(ti + 1) * 512],
                                        start=True, stop=not diag,
                                        skip_group_check=True)
                        if diag:
                            d = jj - 4 * ti
                            last = e.matmul(SSb[par][:, half:half + 512],
                                            sb["ident_b"][:, :],
                                            sb["masks"][:, d * 512:(d + 1) * 512],
                                            start=False, stop=True,
                                            skip_group_check=True)
                    return last
                p.step("pe", smm, reads=qhr + khr + ["masks", "ident_b"],
                       writes=[sspr], tag=f"p8.{h}.{ti}.s{jb}")

                if jb >= 4:
                    mk_avmm8(jb - 4)
                if jb == 2:
                    while deferred8:
                        deferred8.pop(0)()

                p.step("act", lambda e, par=par, pob=pob:
                       e.activation(out=pob[:, :], in_=SSb[par][:, :], func=AF.Exp),
                       reads=[sspr], writes=por, tag=f"p8.{h}.{ti}.e{jb}")

            if nj >= 4:
                mk_avmm8(nj - 4)
            mk_avmm8(nj - 2)

            ot_sb, otr = fview(512, 1024)
            p.step("dve", lambda e: e.tensor_copy(out=ot_sb[:, :], in_=O[:, :]),
                   reads=["psC"], writes=["psC"] + otr, tag=f"p8.{h}.{ti}.e1")
            p.step("dve", lambda e: e.tensor_copy(out=rs_sb[:, :], in_=L[:, :]),
                   reads=["psD"], writes=["psD"] + rsr, tag=f"p8.{h}.{ti}.e2")

            def mk_tail8(h=h, ti=ti):
                p.step("act", lambda e: e.activation(out=rs_sb[:, :], in_=rs_sb[:, :],
                                                     func=AF.Ln),
                       reads=rsr, writes=rsr, tag=f"p8.{h}.{ti}.e3a")
                p.step("act", lambda e: e.activation(out=rs_sb[:, :], in_=rs_sb[:, :],
                                                     func=AF.Exp, scale=-1.0),
                       reads=rsr, writes=rsr, tag=f"p8.{h}.{ti}.e3")
                p.step("pool", lambda e: e.tensor_mul(
                    big0[:, h * S + ti * 512: h * S + (ti + 1) * 512], ot_sb[:, :],
                    rs_sb[:, :]),
                       reads=otr + rsr, writes=[f"o8{h}"], tag=f"p8.{h}.{ti}.e4")
            deferred8.append(mk_tail8)
    while deferred8:
        deferred8.pop(0)()


def _prep_shared(inp):
    f32 = np.float32
    asn = lambda a: np.ascontiguousarray(np.asarray(a, dtype=f32))
    Wq, Wk, Wv, Wo = (asn(inp["L0_attn_Wq"]), asn(inp["L0_attn_Wk"]),
                      asn(inp["L0_attn_Wv"]), asn(inp["L0_attn_Wo"]))
    Wa, Wp = asn(inp["L0_ffn_Wa"]), asn(inp["L0_ffn_Wp"])
    bf = lambda a: np.ascontiguousarray(a).astype(ml_dtypes.bfloat16)

    def blk(WT, ncol):
        nb = WT.shape[1] // ncol
        out = np.empty((nb * 128, 16 * ncol), dtype=ml_dtypes.bfloat16)
        for b in range(nb):
            blkv = WT[:, b * ncol:(b + 1) * ncol].reshape(16, 128, ncol)
            out[b * 128:(b + 1) * 128] = blkv.transpose(1, 0, 2).reshape(128, -1)
        return np.ascontiguousarray(out)
    d = {
        "wqT": blk(bf(Wq.T), 512),
        "wkT": blk(bf(Wk.T), 512),
        "wvT": blk(bf(Wv.T), 512),
        "woT": blk(bf(Wo.T), 256),
        "wp": blk(bf(Wp), 256),
    }
    q_std_end = H * 120
    k_std_end = q_std_end + H * 120
    v_end = k_std_end + D
    q_low_end = v_end + H * 8
    idx_q = np.concatenate([np.concatenate([np.arange(h * 120, (h + 1) * 120),
                                            v_end + np.arange(h * 8, (h + 1) * 8)])
                            for h in range(H)])
    idx_k = np.concatenate([np.concatenate([q_std_end + np.arange(h * 120, (h + 1) * 120),
                                            q_low_end + np.arange(h * 8, (h + 1) * 8)])
                            for h in range(H)])
    d["wa_qa"] = blk(bf(Wa[:, idx_q]), 512)
    d["wa_ka"] = blk(bf(Wa[:, idx_k]), 512)
    d["wa_v"] = blk(bf(Wa[:, k_std_end:v_end]), 512)
    d["wa_low"] = blk(bf(Wa[:, v_end:]), 256)
    d["pos"] = asn(inp["L0_attn_splat_positions"])
    d["dirs"] = asn(inp["L0_attn_splat_directions"])
    d["lsc"] = asn(inp["L0_attn_splat_log_scales"])
    d["lam"] = asn(inp["L0_attn_splat_log_amplitudes"])
    d["ds"] = asn(inp["L0_attn_directional_strength"]).reshape(1, 1)
    d["ps"] = asn(inp["L0_attn_position_sensitivity"]).reshape(1, 1)
    d["gsa"] = asn(inp["L0_attn_gsa_strength"]).reshape(1, 1)
    d["wstd"] = asn(inp["L0_ffn_w_std"]).reshape(1, H)
    d["wrec"] = asn(inp["L0_ffn_w_rec"]).reshape(1, H)
    pvec = np.arange(128)
    cvec = np.arange(512)
    masks = np.zeros((4, 128, 512), dtype=f32)
    for dd in range(4):
        masks[dd] = np.where((128 * dd + pvec[:, None]) <= cvec[None, :], 0.0, NEG)
    d["masksT"] = masks.astype(ml_dtypes.bfloat16)
    d["ident"] = np.eye(128, dtype=f32)
    d["identb"] = np.eye(128).astype(ml_dtypes.bfloat16)
    pats = np.zeros((1, 256), dtype=f32)
    pats[0, 0:120] = 1.0
    pats[0, 248:256] = 1.0
    d["pats"] = pats
    rep = np.zeros((H, 128), dtype=f32)
    for n in range(H):
        rep[n, n * 8:(n + 1) * 8] = 1.0
    d["rep16"] = rep
    return d


def make_in_maps(inputs):
    x = np.asarray(inputs["L0_attn_hidden_states"], dtype=np.float32)
    shared = _prep_shared(inputs)
    in_maps = []
    for b in range(8):
        m = dict(shared)
        xt = np.ascontiguousarray(x[b].T)
        m["xT"] = xt
        m["xTb"] = xt.astype(ml_dtypes.bfloat16)
        in_maps.append(m)
    return in_maps


def kernel(**inputs):
    nc = build_program(taps=tuple(DEBUG_TAPS))
    in_maps = make_in_maps(inputs)
    res = run_bass_kernel_spmd(nc, in_maps, core_ids=list(range(8)))
    out = np.stack([np.ascontiguousarray(np.asarray(r["outT"]).T)
                    for r in res.results], axis=0)
    return out.astype(np.float32)
